# version 06
# speedup vs baseline: 1.5207x; 1.0101x over previous
from contextlib import ExitStack

import numpy as np
import ml_dtypes
import concourse.bass as bass
import concourse.mybir as mybir
from concourse.bass_utils import run_bass_kernel_spmd

F32 = mybir.dt.float32
BF16 = mybir.dt.bfloat16
I32 = mybir.dt.int32
I16 = mybir.dt.int16
ALU = mybir.AluOpType
ACT = mybir.ActivationFunctionType
AXT = mybir.AxisListType

NCORES = 8
S_PC = 2
T = 2048
NT = S_PC * T
D = 1024
DI = 3072
NH = 16
HD = 64
LAT = 32
QMAX = 32767.0
NTT = NT // 128
NDC = D // 128
NIC = DI // 128
LN_EPS = 1e-5

BF16NP = ml_dtypes.bfloat16


def new_nc():
    return bass.Bass("TRN2", target_bir_lowering=False, debug=False,
                     detect_race_conditions=False)


class VSem:

    def __init__(self, h):
        self.h = h
        self.base = 0
        self.count = 0


class InstProxy:
    def __init__(self, inst):
        self._inst = inst

    def then_inc(self, sem, k):
        if isinstance(sem, VSem):
            sem.count += k
            self._inst.then_inc(sem.h, k)
        else:
            self._inst.then_inc(sem, k)
        return self

    def __getattr__(self, name):
        return getattr(self._inst, name)


class EngProxy:
    def __init__(self, eng):
        self._eng = eng

    def wait_ge(self, sem, n):
        if isinstance(sem, VSem):
            self._eng.wait_ge(sem.h, sem.base + n)
        else:
            self._eng.wait_ge(sem, n)

    def __getattr__(self, name):
        f = getattr(self._eng, name)
        if not callable(f):
            return f

        def wrapped(*a, **kw):
            r = f(*a, **kw)
            return InstProxy(r) if r is not None else r
        return wrapped


class FZ:

    ENGS = ("sync", "scalar", "vector", "tensor", "gpsimd")

    def __init__(self, nc, ges):
        self.nc = nc
        self.ges = ges
        self.block = ges.enter_context(nc.Block())
        self.bar = ges.enter_context(nc.semaphore("fz_bar"))
        self.ccd = ges.enter_context(nc.semaphore("fz_ccd"))
        self.bar_count = 0
        self.bar_base = 0
        self.phase = 0
        self.p = "z0_"
        self.ccd_count = 0
        self.cci = []
        self.cco = []
        self.cc_thr = {}
        self._sems = {}

    def make_cc(self, n):
        for k in range(n):
            self.cci.append(self.nc.dram_tensor("fzcci%d" % k, [1, 16], F32))
            self.cco.append(self.nc.dram_tensor("fzcco%d" % k, [1, 16], F32))

    def sem(self, name):
        vs = self._sems.get(name)
        if vs is None:
            vs = VSem(self.ges.enter_context(self.nc.semaphore("fz_" + name)))
            self._sems[name] = vs
        return vs

    def _mk(self, eng_name):
        dec = getattr(self.block, eng_name)
        base = self.bar_base
        bar = self.bar

        def deco(fn):
            @dec
            def _(e):
                if base > 0:
                    e.wait_ge(bar, base)
                fn(EngProxy(e))
                nop = getattr(e, "engine_nop", None) or e.nop
                nop().then_inc(bar, 1)
            self.bar_count += 1
        return deco

    @property
    def sync(self):
        return self._mk("sync")

    @property
    def scalar(self):
        return self._mk("scalar")

    @property
    def vector(self):
        return self._mk("vector")

    @property
    def tensor(self):
        return self._mk("tensor")

    @property
    def gpsimd(self):
        return self._mk("gpsimd")

    def end_phase(self):
        self.bar_base = self.bar_count
        self.phase += 1
        self.p = "z%d_" % self.phase
        for vs in self._sems.values():
            vs.base = vs.count

    def mpart(self, k):
        return (self.cco[k].ap(), self.cc_thr[k])


def emit_cc(fz, store, ks):
    sem_, thr = store
    for k in ks:
        fz.ccd_count += 1
        fz.cc_thr[k] = fz.ccd_count
    ks = list(ks)

    base = sem_.base

    @fz.block.gpsimd
    def _(g):
        g.wait_ge(sem_.h, base + thr)
        for k in ks:
            g.collective_compute(
                "AllReduce", ALU.max,
                replica_groups=[list(range(NCORES))],
                ins=[fz.cci[k].ap().opt()],
                outs=[fz.cco[k].ap().opt()],
            ).then_inc(fz.ccd, 1)


def emit_mparts_load(fz, sync, mpt, srcs, ldm):
    sync.wait_ge(fz.ccd, max(t for _, t in srcs))
    for kk, (src, _) in enumerate(srcs):
        sync.dma_start(out=mpt[:, 8 * kk:8 * kk + 8],
                       in_=src[0:1, 0:8].broadcast_to([128, 8])).then_inc(ldm, 16)


def emit_init(fz):
    nc = fz.nc
    z = fz.ges.enter_context(nc.sbuf_tensor("fz_initz", [1, 16], F32))
    vz = fz.sem("vz")
    n = len(fz.cci)

    @fz.vector
    def _(v):
        v.memset(z[:, :], 0.0).then_inc(vz, 1)

    @fz.sync
    def _(s):
        s.wait_ge(vz, 1)
        for k in range(n):
            s.dma_start(out=fz.cci[k].ap()[:, :], in_=z[0:1, :]).then_inc(vz, 16)
        s.wait_ge(vz, 16 * n + 1)

    fz.end_phase()


class Fence:

    def __init__(self, eng, sem_handle):
        self.eng = eng
        self.sem = sem_handle
        self.n = 0

    def __call__(self, inst=None):
        if inst is not None:
            inst.then_inc(self.sem, 1)
        else:
            self.eng.engine_nop().then_inc(self.sem, 1)
        self.n += 1
        self.eng.wait_ge(self.sem, self.n)


def emit_scale_from_parts(v, m_parts_ap, s_t, r_t, fence):
    fence(v.tensor_reduce(s_t, m_parts_ap, AXT.X, ALU.max))
    fence(v.tensor_scalar(s_t, s_t, 1.0 / QMAX, 1e-12, ALU.mult, ALU.max))
    fence(v.reciprocal(r_t, s_t))


def emit_round(v, x_ap, r_scl, it, ft, qt, ct, out_ap):
    v.tensor_scalar(it, x_ap, r_scl, 65536.0, ALU.mult, ALU.add)
    v.tensor_copy(ft, it)
    v.scalar_tensor_tensor(qt, x_ap, r_scl, ft, ALU.mult, ALU.subtract)
    v.tensor_scalar(ct, qt, -65535.5, None, ALU.is_ge)
    v.scalar_tensor_tensor(out_ap, ft, 65536.0, ct, ALU.subtract, ALU.add)


def emit_colmax_to_scalar(v, colmax_tile, tr32, out11, fence=None):
    if fence is not None:
        fence(None)
    inst = None
    for r in range(4):
        inst = v.transpose(tr32[0:32, 32 * r:32 * r + 32],
                           colmax_tile[32 * r:32 * r + 32, 0:32])
    if fence is not None:
        fence(inst)
    inst = v.tensor_reduce(out11, tr32[0:1, 0:128], AXT.X, ALU.max)
    if fence is not None:
        fence(inst)


def emit_ln_tile(v, sc, x_ap, sq_scr, s1, s2, mv, xn_out_ap):
    v.reduce_sum(s1, x_ap, AXT.X)
    v.tensor_tensor(sq_scr, x_ap, x_ap, ALU.mult)
    v.reduce_sum(s2, sq_scr, AXT.X)
    v.tensor_scalar(mv[:, 0:1], s1, 1.0 / D, None, ALU.mult)
    v.tensor_tensor(mv[:, 1:2], mv[:, 0:1], mv[:, 0:1], ALU.mult)
    v.scalar_tensor_tensor(mv[:, 2:3], s2, 1.0 / D, mv[:, 1:2],
                           ALU.mult, ALU.subtract)
    v.tensor_scalar(mv[:, 2:3], mv[:, 2:3], 1.0, LN_EPS, ALU.mult, ALU.add)
    sc.activation(mv[:, 3:4], mv[:, 2:3], ACT.Sqrt)
    v.reciprocal(mv[:, 4:5], mv[:, 3:4])
    for it in range(2):
        v.tensor_tensor(mv[:, 5:6], mv[:, 4:5], mv[:, 4:5], ALU.mult)
        v.tensor_tensor(mv[:, 5:6], mv[:, 5:6], mv[:, 2:3], ALU.mult)
        v.tensor_scalar(mv[:, 5:6], mv[:, 5:6], -0.5, 1.5, ALU.mult, ALU.add)
        v.tensor_tensor(mv[:, 4:5], mv[:, 4:5], mv[:, 5:6], ALU.mult)
    v.tensor_scalar(xn_out_ap, x_ap, mv[:, 0:1], mv[:, 4:5],
                    ALU.subtract, ALU.mult)



def build_p1(fz, io):
    nc = fz.nc
    x0 = io["x0"]
    wup = io["wup"]
    wga = io["wga"]
    wdo = io["wdo"]
    wupq = io["wupq"]
    wgaq = io["wgaq"]
    wdoq = io["wdoq"]
    xn1 = io["xn1"]
    m1p = io["m1p"]

    weights = [(wup, wupq, DI, D), (wga, wgaq, DI, D), (wdo, wdoq, D, DI)]
    total_wchunks = sum(o // 128 for (_, _, o, _) in weights)

    with ExitStack() as es:
        def sb(name, shape, dt):
            return es.enter_context(nc.sbuf_tensor(fz.p + name, shape, dt))

        def sem(name):
            return fz.sem(name)

        wf = sb("wf", [128, 2 * DI], F32)
        wq = sb("wq", [128, 2 * DI], BF16)
        sg = sb("sg", [128, DI // 32], F32)
        rg = sb("rg", [128, DI // 32], F32)
        wv = sb("wv", [128, DI], F32)
        wi = sb("wi", [128, DI], I32)
        wqs = sb("wqs", [128, DI], F32)
        wcs = sb("wcs", [128, DI], F32)
        wc2 = sb("wc2", [128, DI], F32)
        wif = sb("wif", [128, DI], F32)
        xf = sb("xf", [128, 2 * D], F32)
        xo = sb("xo", [128, 2 * D], BF16)
        sqs = sb("sqs", [128, D], F32)
        s1t = sb("s1t", [128, 1], F32)
        s2t = sb("s2t", [128, 1], F32)
        mvt = sb("mvt", [128, 8], F32)
        mxc = sb("mxc", [128, 32], F32)
        tr32 = sb("tr32", [32, 128], F32)
        m11 = sb("m11", [1, 1], F32)
        ldw = sem("ldw")
        cw = sem("cw")
        sw = sem("sw")
        ldx = sem("ldx")
        cx = sem("cx")
        sx = sem("sx")
        csc = sem("csc")
        fsem = sem("fsem")
        block = fz
        fv_holder = {}

        @block.sync
        def _(sync):
            k = 0
            for (w_in, _, o_dim, in_dim) in weights:
                for r in range(o_dim // 128):
                    if k >= 2:
                        sync.wait_ge(cw, k - 1)
                    sync.dma_start(out=wf[:, (k % 2) * DI:(k % 2) * DI + in_dim],
                                   in_=w_in[128 * r:128 * r + 128, :]).then_inc(ldw, 16)
                    k += 1
            for t in range(NTT):
                if t >= 2:
                    sync.wait_ge(cx, 2 * (t - 2) + 2)
                sync.dma_start(out=xf[:, (t % 2) * D:(t % 2) * D + D],
                               in_=x0[128 * t:128 * t + 128, :]).then_inc(ldx, 16)

        @block.scalar
        def _(sc):
            k = 0
            for (_, w_out, o_dim, in_dim) in weights:
                for r in range(o_dim // 128):
                    sc.wait_ge(cw, k + 1)
                    sc.dma_start(out=w_out[128 * r:128 * r + 128, :],
                                 in_=wq[:, (k % 2) * DI:(k % 2) * DI + in_dim]).then_inc(sw, 16)
                    k += 1
            for t in range(NTT):
                sc.wait_ge(cx, 2 * t + 1)
                sc.activation(mvt[:, 3:4], mvt[:, 2:3], ACT.Sqrt).then_inc(csc, 1)
                sc.wait_ge(cx, 2 * t + 2)
                sc.dma_start(out=xn1[128 * t:128 * t + 128, :],
                             in_=xo[:, (t % 2) * D:(t % 2) * D + D]).then_inc(sx, 16)
            sc.wait_ge(cx, 2 * NTT + 1)
            sc.dma_start(out=m1p[:, :], in_=m11[0:1, 0:1]).then_inc(sx, 16)
            sc.wait_ge(sx, 16 * NTT + 16)
            sc.wait_ge(sw, 16 * total_wchunks)

        @block.vector
        def _(v):
            fv = Fence(v, fsem)
            k = 0
            for (_, _, o_dim, in_dim) in weights:
                G = in_dim // 32
                for r in range(o_dim // 128):
                    v.wait_ge(ldw, 16 * (k + 1))
                    if k >= 2:
                        v.wait_ge(sw, 16 * (k - 1))
                    wfv = wf[:, (k % 2) * DI:(k % 2) * DI + in_dim]
                    wfg = wfv.rearrange("p (g k) -> p g k", k=32)
                    fv(v.tensor_reduce(sg[:, 0:G], wfg, AXT.X, ALU.max,
                                       apply_absolute_value=True))
                    fv(v.tensor_scalar(sg[:, 0:G], sg[:, 0:G], 1.0 / 7.0, 1e-12,
                                       ALU.mult, ALU.max))
                    fv(v.reciprocal(rg[:, 0:G], sg[:, 0:G]))
                    rgb = rg[:, 0:G].rearrange("p (g o) -> p g o", o=1).broadcast_to([128, G, 32])
                    sgb = sg[:, 0:G].rearrange("p (g o) -> p g o", o=1).broadcast_to([128, G, 32])
                    v.tensor_tensor(wv[:, 0:in_dim].rearrange("p (g k) -> p g k", k=32),
                                    wfg, rgb, ALU.mult)
                    v.tensor_scalar(wi[:, 0:in_dim], wv[:, 0:in_dim], 1.0, None,
                                    ALU.mult)
                    v.tensor_copy(wif[:, 0:in_dim], wi[:, 0:in_dim])
                    v.scalar_tensor_tensor(wqs[:, 0:in_dim], wv[:, 0:in_dim], 1.0,
                                           wif[:, 0:in_dim], ALU.mult, ALU.subtract)
                    v.tensor_scalar(wcs[:, 0:in_dim], wqs[:, 0:in_dim], 0.5, None,
                                    ALU.is_ge)
                    v.tensor_scalar(wc2[:, 0:in_dim], wqs[:, 0:in_dim], -1.0, 0.5,
                                    ALU.mult, ALU.is_ge)
                    v.tensor_tensor(wcs[:, 0:in_dim], wcs[:, 0:in_dim],
                                    wc2[:, 0:in_dim], ALU.subtract)
                    v.tensor_tensor(wqs[:, 0:in_dim], wif[:, 0:in_dim],
                                    wcs[:, 0:in_dim], ALU.add)
                    v.tensor_tensor(wq[:, (k % 2) * DI:(k % 2) * DI + in_dim]
                                    .rearrange("p (g k) -> p g k", k=32),
                                    wqs[:, 0:in_dim].rearrange("p (g k) -> p g k", k=32),
                                    sgb, ALU.mult).then_inc(cw, 1)
                    k += 1
            v.memset(mxc[:], 0.0)
            for t in range(NTT):
                v.wait_ge(ldx, 16 * (t + 1))
                if t >= 2:
                    v.wait_ge(sx, 16 * (t - 1))
                xfv = xf[:, (t % 2) * D:(t % 2) * D + D]
                xov = xo[:, (t % 2) * D:(t % 2) * D + D]
                fv(v.reduce_sum(s1t[:, 0:1], xfv, AXT.X))
                v.tensor_tensor(sqs[:, 0:D], xfv, xfv, ALU.mult)
                fv(v.reduce_sum(s2t[:, 0:1], sqs[:, 0:D], AXT.X))
                fv(v.tensor_scalar(mvt[:, 0:1], s1t[:, 0:1], 1.0 / D, None, ALU.mult))
                fv(v.tensor_tensor(mvt[:, 1:2], mvt[:, 0:1], mvt[:, 0:1], ALU.mult))
                fv(v.scalar_tensor_tensor(mvt[:, 2:3], s2t[:, 0:1], 1.0 / D,
                                          mvt[:, 1:2], ALU.mult, ALU.subtract))
                fv(v.tensor_scalar(mvt[:, 2:3], mvt[:, 2:3], 1.0, LN_EPS,
                                   ALU.mult, ALU.add))
                v.engine_nop().then_inc(cx, 1)
                v.wait_ge(csc, t + 1)
                fv(v.reciprocal(mvt[:, 4:5], mvt[:, 3:4]))
                for _ in range(2):
                    fv(v.tensor_tensor(mvt[:, 5:6], mvt[:, 4:5], mvt[:, 4:5],
                                       ALU.mult))
                    fv(v.tensor_tensor(mvt[:, 5:6], mvt[:, 5:6], mvt[:, 2:3],
                                       ALU.mult))
                    fv(v.tensor_scalar(mvt[:, 5:6], mvt[:, 5:6], -0.5, 1.5,
                                       ALU.mult, ALU.add))
                    fv(v.tensor_tensor(mvt[:, 4:5], mvt[:, 4:5], mvt[:, 5:6],
                                       ALU.mult))
                v.tensor_scalar(xov, xfv, mvt[:, 0:1], mvt[:, 4:5],
                                ALU.subtract, ALU.mult)
                fv(v.tensor_reduce(mxc[:, 1:2], xov, AXT.X, ALU.max,
                                   apply_absolute_value=True))
                fv(v.tensor_tensor(mxc[:, 0:1], mxc[:, 0:1], mxc[:, 1:2],
                                   ALU.max))
                v.engine_nop().then_inc(cx, 1)
            emit_colmax_to_scalar(v, mxc, tr32, m11[0:1, 0:1], fence=fv)
            v.engine_nop().then_inc(cx, 1)
    return (sx, 16 * NTT + 16)



def build_p2(fz, io):
    nc = fz.nc
    xn = io["xn"]
    mparts_srcs = io["mparts"]
    wuq = io["wuq"]
    wgq = io["wgq"]
    upr = io["upr"]
    gar = io["gar"]
    m2p = io["m2p"]
    m3p = io["m3p"]
    xqh = io["xqh"]

    NWAVE = 2
    TOKH = NT // NWAVE
    NGC = 2 * NIC

    with ExitStack() as es:
        def sb(name, shape, dt):
            return es.enter_context(nc.sbuf_tensor(fz.p + name, shape, dt))

        def sem(name):
            return fz.sem(name)

        xnt = sb("xnt", [128, 2 * D], BF16)
        xf = sb("xf", [128, D], F32)
        it = sb("it", [128, D], I32)
        ft = sb("ft", [128, D], F32)
        qt = sb("qt", [128, D], F32)
        ct = sb("ct", [128, D], F32)
        s1 = sb("s1", [128, 1], F32)
        r1 = sb("r1", [128, 1], F32)
        mpt = sb("mpt", [128, 8], F32)
        wTu = sb("wTu", [128, NDC * DI], BF16)
        wTg = sb("wTg", [128, NDC * DI], BF16)
        xqT = sb("xqT", [128, NDC * NT], BF16)
        spb = sb("spb", [128, 2 * NT], BF16)
        mxc = sb("mxc", [128, 40], F32)
        tr32 = sb("tr32", [32, 128], F32)
        m11 = sb("m11", [1, 2], F32)
        banks = [es.enter_context(nc.psum_tensor(fz.p + "pb%d" % i, [128, 512], F32))
                 for i in range(8)]
        ldm = sem("ldm")
        ldx = sem("ldx")
        aq = sem("aq")
        sxq = sem("sxq")
        txq = sem("txq")
        twq = sem("twq")
        mmx = sem("mmx")
        evc = sem("evc")
        spl = sem("spl")
        dvr = sem("dvr")
        fsv = sem("fsv")
        fsc = sem("fsc")
        don = sem("don")
        block = fz

        @block.sync
        def _(sync):
            emit_mparts_load(fz, sync, mpt, mparts_srcs, ldm)
            for wi, (w_in, wt) in enumerate(((wuq, wTu), (wgq, wTg))):
                for j in range(NDC):
                    sync.dma_start_transpose(wt[:, j * DI:(j + 1) * DI],
                                             w_in[:, 128 * j:128 * j + 128]).then_inc(twq, 16)
            for t in range(NTT):
                if t >= 2:
                    sync.wait_ge(sxq, 16 * (t - 1))
                sync.dma_start(out=xnt[:, (t % 2) * D:(t % 2) * D + D],
                               in_=xn[128 * t:128 * t + 128, :]).then_inc(ldx, 16)
            sync.wait_ge(sxq, 16 * NTT)
            for j in range(NDC):
                sync.dma_start_transpose(xqT[:, j * NT:(j + 1) * NT],
                                         xqh[:, 128 * j:128 * j + 128]).then_inc(txq, 16)
            for gc in range(NGC):
                w_out = upr if gc < NIC else gar
                c = gc % NIC
                sync.wait_ge(evc, NWAVE * gc + NWAVE)
                sync.dma_start(out=w_out[128 * c:128 * c + 128, :],
                               in_=spb[:, (gc % 2) * NT:(gc % 2) * NT + NT]).then_inc(spl, 16)
            sync.wait_ge(don, 1)
            sync.dma_start(out=m2p[:, :], in_=m11[0:1, 0:1]).then_inc(spl, 16)
            sync.dma_start(out=m3p[:, :], in_=m11[0:1, 1:2]).then_inc(spl, 16)
            sync.wait_ge(spl, 16 * NGC + 32)

        @block.scalar
        def _(sc):
            sfc = [0]

            def fsc_f(inst):
                (inst if inst is not None else sc.nop()).then_inc(fsc, 1)
                sfc[0] += 1
                sc.wait_ge(fsc, sfc[0])

            for t in range(NTT):
                sc.wait_ge(aq, t + 1)
                sc.dma_start(out=xqh[128 * t:128 * t + 128, :],
                             in_=xnt[:, (t % 2) * D:(t % 2) * D + D]).then_inc(sxq, 16)
            for gc in range(NGC):
                for wave in range(NWAVE):
                    widx = gc * NWAVE + wave
                    sc.wait_ge(mmx, widx + 1)
                    if wave == 0 and gc >= 2:
                        sc.wait_ge(spl, 16 * (gc - 1))
                        sc.wait_ge(dvr, gc - 1)
                    inst = None
                    for tt in range(4):
                        bk = banks[wave * 4 + tt]
                        col = (gc % 2) * NT + wave * TOKH + tt * 512
                        inst = sc.activation(spb[:, col:col + 512], bk[:, :], ACT.Copy)
                    fsc_f(inst)
                    sc.nop().then_inc(evc, 1)

        @block.vector
        def _(v):
            fc = [0]

            def fv(inst):
                (inst if inst is not None else v.engine_nop()).then_inc(fsv, 1)
                fc[0] += 1
                v.wait_ge(fsv, fc[0])

            v.wait_ge(ldm, 16)
            emit_scale_from_parts(v, mpt[:, 0:8], s1[:, 0:1], r1[:, 0:1], fv)
            fv(v.memset(mxc[:, 0:40], 0.0))
            for t in range(NTT):
                v.wait_ge(ldx, 16 * (t + 1))
                xnv = xnt[:, (t % 2) * D:(t % 2) * D + D]
                v.tensor_scalar(xf[:, :], xnv, 1.0, None, ALU.mult)
                emit_round(v, xf[:, :], r1[:, 0:1], it[:, :], ft[:, :], qt[:, :],
                           ct[:, :], qt[:, :])
                v.tensor_scalar(xnv, qt[:, :], s1[:, 0:1], None,
                                ALU.mult).then_inc(aq, 1)
            for gc in range(NGC):
                acc = 0 if gc < NIC else 2
                v.wait_ge(evc, NWAVE * gc + NWAVE)
                fv(v.tensor_reduce(mxc[:, 4 + acc:5 + acc],
                                   spb[:, (gc % 2) * NT:(gc % 2) * NT + NT],
                                   AXT.X, ALU.max, apply_absolute_value=True))
                fv(v.tensor_tensor(mxc[:, acc:acc + 1], mxc[:, acc:acc + 1],
                                   mxc[:, 4 + acc:5 + acc], ALU.max))
                v.engine_nop().then_inc(dvr, 1)
            emit_colmax_to_scalar(v, mxc[:, 0:32], tr32, m11[0:1, 0:1], fence=fv)
            emit_colmax_to_scalar(v, mxc[:, 2:34], tr32, m11[0:1, 1:2], fence=fv)
            v.engine_nop().then_inc(don, 1)

        @block.tensor
        def _(te):
            te.wait_ge(txq, 16 * NDC)
            te.wait_ge(twq, 16 * 2 * NDC)
            for gc in range(NGC):
                wtile = wTu if gc < NIC else wTg
                c = gc % NIC
                for wave in range(NWAVE):
                    widx = gc * NWAVE + wave
                    if widx >= 2:
                        te.wait_ge(evc, widx - 1)
                    inst = None
                    for j in range(NDC):
                        lhs = wtile[:, j * DI + 128 * c:j * DI + 128 * c + 128]
                        for tt in range(4):
                            tchunk = wave * 4 + tt
                            rhs = xqT[:, j * NT + 512 * tchunk:j * NT + 512 * tchunk + 512]
                            inst = te.matmul(banks[wave * 4 + tt][:, :], lhs, rhs,
                                             start=(j == 0), stop=(j == NDC - 1))
                    inst.then_inc(mmx, 1)
    return (spl, 16 * NGC + 32)



def _emit_p3_half(e, is_pool, lo, hi, gin_v, A, B, C, G, I1, r3):
    s = slice(lo, hi)
    if is_pool:
        e.tensor_scalar(A[:, s], gin_v[:, s], r3, 0.0, ALU.mult, ALU.add)
    else:
        e.tensor_scalar(A[:, s], gin_v[:, s], r3, None, ALU.mult)
    e.tensor_scalar(I1[:, s], A[:, s], 1.0, 65536.0, ALU.mult, ALU.add)
    e.tensor_scalar(B[:, s], I1[:, s], 1.0, -65535.5, ALU.mult, ALU.add)
    if is_pool:
        e.tensor_tensor(C[:, s], A[:, s], B[:, s], ALU.subtract)
        e.tensor_scalar(C[:, s], C[:, s], 0.0, None, ALU.is_ge)
        e.tensor_tensor(G[:, s], B[:, s], C[:, s], ALU.add)
        e.tensor_scalar(G[:, s], G[:, s], 1.0, -0.5, ALU.mult, ALU.add)
    else:
        e.tensor_tensor(C[:, s], A[:, s], B[:, s], ALU.is_ge)
        e.scalar_tensor_tensor(G[:, s], C[:, s], -0.5, B[:, s], ALU.add, ALU.add)


def _emit_p3_half2(e, is_pool, lo, hi, Dv, E, U, I1, S, G, Pout_v):
    s = slice(lo, hi)
    if is_pool:
        e.tensor_scalar(I1[:, s], S[:, s], 65536.0, 0.0, ALU.mult, ALU.add)
        e.tensor_scalar(Dv[:, s], I1[:, s], 1.0, 0.0, ALU.mult, ALU.add)
    else:
        e.tensor_scalar(I1[:, s], S[:, s], 65536.0, None, ALU.mult)
        e.tensor_scalar(Dv[:, s], I1[:, s], 1.0, None, ALU.mult)
    if is_pool:
        e.tensor_scalar(E[:, s], S[:, s], 65536.0, 0.0, ALU.mult, ALU.add)
        e.tensor_tensor(E[:, s], E[:, s], Dv[:, s], ALU.subtract)
        e.tensor_scalar(E[:, s], E[:, s], 0.5, None, ALU.is_ge)
        e.tensor_tensor(U[:, s], Dv[:, s], E[:, s], ALU.add)
        e.tensor_scalar(U[:, s], U[:, s], 65535.0, None, ALU.min)
        e.tensor_tensor(Pout_v[:, s], U[:, s], G[:, s], ALU.mult)
    else:
        e.scalar_tensor_tensor(E[:, s], S[:, s], 65536.0, Dv[:, s],
                               ALU.mult, ALU.subtract)
        e.tensor_scalar(E[:, s], E[:, s], 0.5, None, ALU.is_ge)
        e.tensor_tensor(U[:, s], E[:, s], Dv[:, s], ALU.add)
        e.scalar_tensor_tensor(Pout_v[:, s], U[:, s], 65535.0, G[:, s],
                               ALU.min, ALU.mult)


def build_p3(fz, io):
    nc = fz.nc
    gar = io["gar"]
    mparts_srcs = io["mparts"]
    g1s = io["g1s"]
    m4p = io["m4p"]

    SPLIT = 3008

    with ExitStack() as es:
        def sb(name, shape, dt):
            return es.enter_context(nc.sbuf_tensor(fz.p + name, shape, dt))

        def sem(name):
            return fz.sem(name)

        gin = sb("gin", [128, 2 * NT], BF16)
        A = sb("A", [128, NT], F32)
        B = sb("B", [128, NT], F32)
        C = sb("C", [128, NT], F32)
        G = sb("G", [128, NT], F32)
        Dv = sb("Dv", [128, NT], F32)
        E = sb("E", [128, NT], F32)
        U = sb("U", [128, NT], F32)
        I1 = sb("I1", [128, NT], I32)
        S = sb("S", [128, NT], F32)
        Pout = sb("Pout", [128, 2 * NT], BF16)
        s3 = sb("s3", [128, 1], F32)
        r3 = sb("r3", [128, 1], F32)
        mpt = sb("mpt", [128, 8], F32)
        mxc = sb("mxc", [128, 40], F32)
        tr32 = sb("tr32", [32, 128], F32)
        m11 = sb("m11", [1, 1], F32)
        ldm = sem("ldm")
        ldg = sem("ldg")
        scl = sem("scl")
        vG = sem("vG")
        gG = sem("gG")
        ssg = sem("ssg")
        vP = sem("vP")
        gP = sem("gP")
        dvr = sem("dvr")
        spl = sem("spl")
        fsv = sem("fsv")
        fsg = sem("fsg")
        don = sem("don")
        block = fz

        @block.sync
        def _(sync):
            emit_mparts_load(fz, sync, mpt, mparts_srcs, ldm)
            for c in range(NIC + 2):
                if 2 <= c:
                    sync.wait_ge(vP, c - 1)
                    sync.wait_ge(gP, c - 1)
                    sync.dma_start(out=g1s[128 * (c - 2):128 * (c - 2) + 128, :],
                                   in_=Pout[:, (c % 2) * NT:(c % 2) * NT + NT]).then_inc(spl, 16)
                if c < NIC:
                    sync.dma_start(out=gin[:, (c % 2) * NT:(c % 2) * NT + NT],
                                   in_=gar[128 * c:128 * c + 128, :]).then_inc(ldg, 16)
            sync.wait_ge(don, 1)
            sync.dma_start(out=m4p[:, :], in_=m11[0:1, 0:1]).then_inc(spl, 16)
            sync.wait_ge(spl, 16 * NIC + 16)

        @block.scalar
        def _(sc):
            for c in range(NIC):
                sc.wait_ge(vG, c + 1)
                sc.wait_ge(gG, c + 1)
                sc.activation(S[:, :], G[:, :], ACT.Sigmoid,
                              scale=s3[:, 0:1]).then_inc(ssg, 1)

        @block.vector
        def _(v):
            fc = [0]

            def fv(inst):
                (inst if inst is not None else v.engine_nop()).then_inc(fsv, 1)
                fc[0] += 1
                v.wait_ge(fsv, fc[0])

            v.wait_ge(ldm, 16)
            emit_scale_from_parts(v, mpt[:, 0:8], s3[:, 0:1], r3[:, 0:1], fv)
            fv(v.memset(mxc[:, 0:40], 0.0))
            v.engine_nop().then_inc(scl, 1)
            for c in range(NIC):
                v.wait_ge(ldg, 16 * (c + 1))
                if c >= 2:
                    v.wait_ge(spl, 16 * (c - 1))
                    v.wait_ge(dvr, c - 1)
                gv = gin[:, (c % 2) * NT:(c % 2) * NT + NT]
                pv = Pout[:, (c % 2) * NT:(c % 2) * NT + NT]
                _emit_p3_half(v, False, 0, SPLIT, gv, A, B, C, G, I1, r3[:, 0:1])
                v.engine_nop().then_inc(vG, 1)
                v.wait_ge(ssg, c + 1)
                _emit_p3_half2(v, False, 0, SPLIT, Dv, E, U, I1, S, G, pv)
                v.engine_nop().then_inc(vP, 1)
                v.wait_ge(gP, c + 1)
                fv(v.tensor_reduce(mxc[:, 4:5], pv, AXT.X, ALU.max,
                                   apply_absolute_value=True))
                fv(v.tensor_tensor(mxc[:, 0:1], mxc[:, 0:1], mxc[:, 4:5], ALU.max))
                v.engine_nop().then_inc(dvr, 1)
            emit_colmax_to_scalar(v, mxc[:, 0:32], tr32, m11[0:1, 0:1], fence=fv)
            v.engine_nop().then_inc(don, 1)

        @block.gpsimd
        def _(g):
            gfc = [0]

            def fg(inst):
                (inst if inst is not None else g.engine_nop()).then_inc(fsg, 1)
                gfc[0] += 1
                g.wait_ge(fsg, gfc[0])

            g.wait_ge(scl, 1)
            for c in range(NIC):
                g.wait_ge(ldg, 16 * (c + 1))
                if c >= 2:
                    g.wait_ge(spl, 16 * (c - 1))
                    g.wait_ge(dvr, c - 1)
                gv = gin[:, (c % 2) * NT:(c % 2) * NT + NT]
                pv = Pout[:, (c % 2) * NT:(c % 2) * NT + NT]
                _emit_p3_half(g, True, SPLIT, NT, gv, A, B, C, G, I1, r3[:, 0:1])
                g.engine_nop().then_inc(gG, 1)
                g.wait_ge(ssg, c + 1)
                _emit_p3_half2(g, True, SPLIT, NT, Dv, E, U, I1, S, G, pv)
                g.engine_nop().then_inc(gP, 1)
    return (spl, 16 * NIC + 16)



def _emit_round_half(e, is_pool, lo, hi, x_v, r_scl, A, B, C, I1, OUT):
    s = slice(lo, hi)
    if is_pool:
        e.tensor_scalar(A[:, s], x_v[:, s], r_scl, 0.0, ALU.mult, ALU.add)
    else:
        e.tensor_scalar(A[:, s], x_v[:, s], r_scl, None, ALU.mult)
    e.tensor_scalar(I1[:, s], A[:, s], 1.0, 65536.0, ALU.mult, ALU.add)
    e.tensor_scalar(B[:, s], I1[:, s], 1.0, -65535.5, ALU.mult, ALU.add)
    if is_pool:
        e.tensor_tensor(C[:, s], A[:, s], B[:, s], ALU.subtract)
        e.tensor_scalar(C[:, s], C[:, s], 0.0, None, ALU.is_ge)
        e.tensor_tensor(OUT[:, s], B[:, s], C[:, s], ALU.add)
        e.tensor_scalar(OUT[:, s], OUT[:, s], 1.0, -0.5, ALU.mult, ALU.add)
    else:
        e.tensor_tensor(C[:, s], A[:, s], B[:, s], ALU.is_ge)
        e.scalar_tensor_tensor(OUT[:, s], C[:, s], -0.5, B[:, s], ALU.add, ALU.add)


def _emit_p4_scales(v, fv, mpt, s2, r2, s3k, r4P, tmp):
    emit_scale_from_parts(v, mpt[:, 0:8], s2[:, 0:1], r2[:, 0:1], fv)
    emit_scale_from_parts(v, mpt[:, 8:16], s3k[:, 0:1], tmp[:, 0:1], fv)
    fv(v.tensor_scalar(s3k[:, 0:1], s3k[:, 0:1], 1.0 / 65536.0, None, ALU.mult))
    fv(v.tensor_reduce(tmp[:, 1:2], mpt[:, 16:24], AXT.X, ALU.max))
    fv(v.tensor_tensor(tmp[:, 1:2], tmp[:, 1:2], s3k[:, 0:1], ALU.mult))
    fv(v.tensor_scalar(tmp[:, 1:2], tmp[:, 1:2], 1.0 / QMAX, 1e-12,
                       ALU.mult, ALU.max))
    fv(v.reciprocal(tmp[:, 2:3], tmp[:, 1:2]))
    fv(v.tensor_tensor(r4P[:, 0:1], s3k[:, 0:1], tmp[:, 2:3], ALU.mult))


def build_p4(fz, io):
    nc = fz.nc
    g1s = io["g1s"]
    upr = io["upr"]
    mparts_srcs = io["mparts"]
    m5p = io["m5p"]
    i4h = io["i4h"]
    i2h = io["i2h"]

    SPLIT = 2688

    with ExitStack() as es:
        def sb(name, shape, dt):
            return es.enter_context(nc.sbuf_tensor(fz.p + name, shape, dt))

        def sem(name):
            return fz.sem(name)

        pin = sb("pin", [128, 2 * NT], BF16)
        uin = sb("uin", [128, 2 * NT], BF16)
        i4o = sb("i4o", [128, 2 * NT], I16)
        i2o = sb("i2o", [128, 2 * NT], I16)
        A = sb("A", [128, NT], F32)
        B = sb("B", [128, NT], F32)
        C = sb("C", [128, NT], F32)
        G4 = sb("G4", [128, NT], F32)
        U = sb("U", [128, NT], F32)
        I1 = sb("I1", [128, NT], I32)
        s2 = sb("s2", [128, 1], F32)
        r2 = sb("r2", [128, 1], F32)
        s3k = sb("s3k", [128, 1], F32)
        r4P = sb("r4P", [128, 1], F32)
        tmp = sb("tmp", [128, 4], F32)
        mpt = sb("mpt", [128, 24], F32)
        mxc = sb("mxc", [128, 40], F32)
        tr32 = sb("tr32", [32, 128], F32)
        m11 = sb("m11", [1, 1], F32)
        ldm = sem("ldm")
        ldp = sem("ldp")
        ldu = sem("ldu")
        scl = sem("scl")
        vD = sem("vD")
        gD = sem("gD")
        dvr = sem("dvr")
        fsv = sem("fsv")
        fsg = sem("fsg")
        don = sem("don")
        spl = sem("spl")
        block = fz

        @block.sync
        def _(sync):
            emit_mparts_load(fz, sync, mpt, mparts_srcs, ldm)
            for c in range(NIC + 2):
                if c >= 2:
                    sync.wait_ge(vD, c - 1)
                    sync.wait_ge(gD, c - 1)
                    sync.dma_start(out=i4h[128 * (c - 2):128 * (c - 2) + 128, :],
                                   in_=i4o[:, (c % 2) * NT:(c % 2) * NT + NT]).then_inc(spl, 16)
                    sync.dma_start(out=i2h[128 * (c - 2):128 * (c - 2) + 128, :],
                                   in_=i2o[:, (c % 2) * NT:(c % 2) * NT + NT]).then_inc(spl, 16)
                if c < NIC:
                    sync.dma_start(out=pin[:, (c % 2) * NT:(c % 2) * NT + NT],
                                   in_=g1s[128 * c:128 * c + 128, :]).then_inc(ldp, 16)
                    sync.dma_start(out=uin[:, (c % 2) * NT:(c % 2) * NT + NT],
                                   in_=upr[128 * c:128 * c + 128, :]).then_inc(ldu, 16)
            sync.wait_ge(don, 1)
            sync.dma_start(out=m5p[:, :], in_=m11[0:1, 0:1]).then_inc(spl, 16)
            sync.wait_ge(spl, 32 * NIC + 16)

        @block.vector
        def _(v):
            fc = [0]

            def fv(inst):
                (inst if inst is not None else v.engine_nop()).then_inc(fsv, 1)
                fc[0] += 1
                v.wait_ge(fsv, fc[0])

            v.wait_ge(ldm, 48)
            _emit_p4_scales(v, fv, mpt, s2, r2, s3k, r4P, tmp)
            fv(v.memset(mxc[:, 0:40], 0.0))
            v.engine_nop().then_inc(scl, 1)
            for c in range(NIC):
                v.wait_ge(ldp, 16 * (c + 1))
                v.wait_ge(ldu, 16 * (c + 1))
                if c >= 2:
                    v.wait_ge(spl, 32 * (c - 1))
                pv = pin[:, (c % 2) * NT:(c % 2) * NT + NT]
                uv = uin[:, (c % 2) * NT:(c % 2) * NT + NT]
                i4v = i4o[:, (c % 2) * NT:(c % 2) * NT + NT]
                i2v = i2o[:, (c % 2) * NT:(c % 2) * NT + NT]
                _emit_round_half(v, False, 0, SPLIT, pv, r4P[:, 0:1], A, B, C, I1, G4)
                _emit_round_half(v, False, 0, SPLIT, uv, r2[:, 0:1], A, B, C, I1, U)
                v.tensor_scalar(i4v[:, 0:SPLIT], G4[:, 0:SPLIT], 1.0, None, ALU.mult)
                v.tensor_scalar(i2v[:, 0:SPLIT], U[:, 0:SPLIT], 1.0, None, ALU.mult)
                v.tensor_tensor(A[:, 0:SPLIT], G4[:, 0:SPLIT], U[:, 0:SPLIT],
                                ALU.mult)
                v.engine_nop().then_inc(vD, 1)
                v.wait_ge(gD, c + 1)
                fv(v.tensor_reduce(mxc[:, 4:5], A[:, :], AXT.X, ALU.max,
                                   apply_absolute_value=True))
                fv(v.tensor_tensor(mxc[:, 0:1], mxc[:, 0:1], mxc[:, 4:5], ALU.max))
                v.engine_nop().then_inc(dvr, 1)
            emit_colmax_to_scalar(v, mxc[:, 0:32], tr32, m11[0:1, 0:1], fence=fv)
            v.engine_nop().then_inc(don, 1)

        @block.gpsimd
        def _(g):
            gfc = [0]

            def fg(inst):
                (inst if inst is not None else g.engine_nop()).then_inc(fsg, 1)
                gfc[0] += 1
                g.wait_ge(fsg, gfc[0])

            g.wait_ge(scl, 1)
            for c in range(NIC):
                g.wait_ge(ldp, 16 * (c + 1))
                g.wait_ge(ldu, 16 * (c + 1))
                if c >= 1:
                    g.wait_ge(dvr, c)
                if c >= 2:
                    g.wait_ge(spl, 32 * (c - 1))
                pv = pin[:, (c % 2) * NT:(c % 2) * NT + NT]
                uv = uin[:, (c % 2) * NT:(c % 2) * NT + NT]
                i4v = i4o[:, (c % 2) * NT:(c % 2) * NT + NT]
                i2v = i2o[:, (c % 2) * NT:(c % 2) * NT + NT]
                _emit_round_half(g, True, SPLIT, NT, pv, r4P[:, 0:1], A, B, C, I1, G4)
                _emit_round_half(g, True, SPLIT, NT, uv, r2[:, 0:1], A, B, C, I1, U)
                g.tensor_scalar(i4v[:, SPLIT:NT], G4[:, SPLIT:NT], 1.0, 0.0,
                                ALU.mult, ALU.add)
                g.tensor_scalar(i2v[:, SPLIT:NT], U[:, SPLIT:NT], 1.0, 0.0,
                                ALU.mult, ALU.add)
                g.tensor_tensor(A[:, SPLIT:NT], G4[:, SPLIT:NT], U[:, SPLIT:NT],
                                ALU.mult)
                g.engine_nop().then_inc(gD, 1)
    return (spl, 32 * NIC + 16)



def build_p5a(fz, io, final):
    nc = fz.nc
    i4h = io["i4h"]
    i2h = io["i2h"]
    mparts_srcs = io["mparts"]
    wdoq = io["wdoq"]
    xres = io["xres"]
    xout = io["xout"]
    if not final:
        xnout = io["xnout"]
    obh = io["obh"]

    SPLIT = 2624

    with ExitStack() as es:
        def sb(name, shape, dt):
            return es.enter_context(nc.sbuf_tensor(fz.p + name, shape, dt))

        def sem(name):
            return fz.sem(name)

        p4in = sb("p4in", [128, 2 * NT], I16)
        u4in = sb("u4in", [128, 2 * NT], I16)
        A = sb("A", [128, NT], F32)
        B = sb("B", [128, NT], F32)
        C = sb("C", [128, NT], F32)
        OI = sb("OI", [128, NT], F32)
        I1 = sb("I1", [128, NT], I32)
        ob = sb("ob", [128, 2 * NT], BF16)
        s5 = sb("s5", [128, 1], F32)
        r5P = sb("r5P", [128, 1], F32)
        s2 = sb("s2", [128, 1], F32)
        r2u = sb("r2u", [128, 1], F32)
        s3k = sb("s3k", [128, 1], F32)
        tmp = sb("tmp", [128, 8], F32)
        mpt = sb("mpt", [128, 32], F32)
        wdT = sb("wdT", [128, NIC * D], BF16)
        obt = sb("obt", [128, 2 * NIC * 128], BF16)
        xot = sb("xot", [128, 2 * D], BF16)
        s1t = sb("s1t", [128, 1], F32)
        s2t = sb("s2t", [128, 1], F32)
        mvt = sb("mvt", [128, 8], F32)
        banks = [es.enter_context(nc.psum_tensor(fz.p + "pb%d" % i, [128, 512], F32))
                 for i in range(4)]
        xrt = A[:, 0:2 * D]
        x1t = A[:, 2 * D:4 * D]
        sqs = C[:, 0:D]
        ldm = sem("ldm")
        ldp = sem("ldp")
        vD = sem("vD")
        gD = sem("gD")
        spo = sem("spo")
        twd = sem("twd")
        ldo = sem("ldo")
        ldx = sem("ldx")
        mmx = sem("mmx")
        swp = sem("swp")
        sx1 = sem("sx1")
        sxn = sem("sxn")
        csc = sem("csc")
        fsv = sem("fsv")
        fsg = sem("fsg")
        scl = sem("scl")
        block = fz

        @block.sync
        def _(sync):
            emit_mparts_load(fz, sync, mpt, mparts_srcs, ldm)
            for j in range(NIC):
                sync.dma_start_transpose(wdT[:, j * D:(j + 1) * D],
                                         wdoq[:, 128 * j:128 * j + 128]).then_inc(twd, 16)
            for c in range(NIC + 2):
                if c >= 2:
                    sync.wait_ge(vD, c - 1)
                    sync.wait_ge(gD, c - 1)
                    sync.dma_start(out=obh[128 * (c - 2):128 * (c - 2) + 128, :],
                                   in_=ob[:, (c % 2) * NT:(c % 2) * NT + NT]).then_inc(spo, 16)
                if c < NIC:
                    sync.dma_start(out=p4in[:, (c % 2) * NT:(c % 2) * NT + NT],
                                   in_=i4h[128 * c:128 * c + 128, :]).then_inc(ldp, 16)
                    sync.dma_start(out=u4in[:, (c % 2) * NT:(c % 2) * NT + NT],
                                   in_=i2h[128 * c:128 * c + 128, :]).then_inc(ldp, 16)
            sync.wait_ge(spo, 16 * NIC)
            for t in range(NTT):
                if t >= 2:
                    sync.wait_ge(mmx, t - 1)
                for ci in range(NIC):
                    sync.dma_start(out=obt[:, (t % 2) * NIC * 128 + 128 * ci:
                                           (t % 2) * NIC * 128 + 128 * ci + 128],
                                   in_=obh[128 * ci:128 * ci + 128,
                                           128 * t:128 * t + 128]).then_inc(ldo, 16)
                if t >= 2:
                    sync.wait_ge(swp, t - 1)
                sync.dma_start(out=xrt[:, (t % 2) * D:(t % 2) * D + D],
                               in_=xres[128 * t:128 * t + 128, :]).then_inc(ldx, 16)
            sync.wait_ge(sx1, 16 * NTT)
            if not final:
                sync.wait_ge(sxn, 16 * NTT)

        @block.scalar
        def _(sc):
            for t in range(NTT):
                if final:
                    sc.wait_ge(swp, t + 1)
                    sc.dma_start(out=xout[128 * t:128 * t + 128, :],
                                 in_=x1t[:, (t % 2) * D:(t % 2) * D + D]).then_inc(sx1, 16)
                else:
                    sc.wait_ge(swp, 3 * t + 1)
                    sc.dma_start(out=xout[128 * t:128 * t + 128, :],
                                 in_=x1t[:, (t % 2) * D:(t % 2) * D + D]).then_inc(sx1, 16)
                    sc.wait_ge(swp, 3 * t + 2)
                    sc.activation(mvt[:, 3:4], mvt[:, 2:3], ACT.Sqrt).then_inc(csc, 1)
                    sc.wait_ge(swp, 3 * t + 3)
                    sc.dma_start(out=xnout[128 * t:128 * t + 128, :],
                                 in_=xot[:, (t % 2) * D:(t % 2) * D + D]).then_inc(sxn, 16)

        @block.vector
        def _(v):
            fc = [0]

            def fv(inst):
                (inst if inst is not None else v.engine_nop()).then_inc(fsv, 1)
                fc[0] += 1
                v.wait_ge(fsv, fc[0])

            v.wait_ge(ldm, 64)
            emit_scale_from_parts(v, mpt[:, 0:8], s2[:, 0:1], r2u[:, 0:1], fv)
            emit_scale_from_parts(v, mpt[:, 8:16], s3k[:, 0:1], tmp[:, 3:4], fv)
            fv(v.tensor_scalar(s3k[:, 0:1], s3k[:, 0:1], 1.0 / 65536.0, None, ALU.mult))
            fv(v.tensor_reduce(tmp[:, 1:2], mpt[:, 16:24], AXT.X, ALU.max))
            fv(v.tensor_tensor(tmp[:, 1:2], tmp[:, 1:2], s3k[:, 0:1], ALU.mult))
            fv(v.tensor_scalar(tmp[:, 1:2], tmp[:, 1:2], 1.0 / QMAX, 1e-12,
                               ALU.mult, ALU.max))
            fv(v.tensor_tensor(tmp[:, 4:5], tmp[:, 1:2], s2[:, 0:1], ALU.mult))
            fv(v.tensor_reduce(tmp[:, 5:6], mpt[:, 24:32], AXT.X, ALU.max))
            fv(v.tensor_tensor(tmp[:, 5:6], tmp[:, 5:6], tmp[:, 4:5], ALU.mult))
            fv(v.tensor_scalar(s5[:, 0:1], tmp[:, 5:6], 1.0 / QMAX, 1e-12,
                               ALU.mult, ALU.max))
            fv(v.reciprocal(tmp[:, 6:7], s5[:, 0:1]))
            fv(v.tensor_tensor(r5P[:, 0:1], tmp[:, 4:5], tmp[:, 6:7], ALU.mult))
            v.engine_nop().then_inc(scl, 1)
            for c in range(NIC):
                v.wait_ge(ldp, 32 * (c + 1))
                if c >= 2:
                    v.wait_ge(spo, 16 * (c - 1))
                pv = p4in[:, (c % 2) * NT:(c % 2) * NT + NT]
                uv = u4in[:, (c % 2) * NT:(c % 2) * NT + NT]
                obv = ob[:, (c % 2) * NT:(c % 2) * NT + NT]
                sl = slice(0, SPLIT)
                v.tensor_tensor(A[:, sl], pv[:, sl], uv[:, sl], ALU.mult)
                _emit_round_half(v, False, 0, SPLIT, A, r5P[:, 0:1], B, C, A, I1, OI)
                v.tensor_scalar(obv[:, sl], OI[:, sl], s5[:, 0:1], None, ALU.mult)
                v.engine_nop().then_inc(vD, 1)
            for t in range(NTT):
                v.wait_ge(mmx, t + 1)
                v.wait_ge(ldx, 16 * (t + 1))
                if t >= 2:
                    v.wait_ge(sx1, 16 * (t - 1))
                    if not final:
                        v.wait_ge(sxn, 16 * (t - 1))
                xrv = xrt[:, (t % 2) * D:(t % 2) * D + D]
                x1v = x1t[:, (t % 2) * D:(t % 2) * D + D]
                xov = xot[:, (t % 2) * D:(t % 2) * D + D]
                for h in range(2):
                    bk = banks[(t % 2) * 2 + h]
                    v.scalar_tensor_tensor(x1v[:, 512 * h:512 * h + 512], bk[:, :],
                                           0.5, xrv[:, 512 * h:512 * h + 512],
                                           ALU.mult, ALU.add)
                v.engine_nop().then_inc(swp, 1)
                if final:
                    continue
                fv(v.reduce_sum(s1t[:, 0:1], x1v, AXT.X))
                v.tensor_tensor(sqs[:, 0:D], x1v, x1v, ALU.mult)
                fv(v.reduce_sum(s2t[:, 0:1], sqs[:, 0:D], AXT.X))
                fv(v.tensor_scalar(mvt[:, 0:1], s1t[:, 0:1], 1.0 / D, None, ALU.mult))
                fv(v.tensor_tensor(mvt[:, 1:2], mvt[:, 0:1], mvt[:, 0:1], ALU.mult))
                fv(v.scalar_tensor_tensor(mvt[:, 2:3], s2t[:, 0:1], 1.0 / D,
                                          mvt[:, 1:2], ALU.mult, ALU.subtract))
                fv(v.tensor_scalar(mvt[:, 2:3], mvt[:, 2:3], 1.0, LN_EPS,
                                   ALU.mult, ALU.add))
                v.engine_nop().then_inc(swp, 1)
                v.wait_ge(csc, t + 1)
                fv(v.reciprocal(mvt[:, 4:5], mvt[:, 3:4]))
                for _ in range(2):
                    fv(v.tensor_tensor(mvt[:, 5:6], mvt[:, 4:5], mvt[:, 4:5], ALU.mult))
                    fv(v.tensor_tensor(mvt[:, 5:6], mvt[:, 5:6], mvt[:, 2:3], ALU.mult))
                    fv(v.tensor_scalar(mvt[:, 5:6], mvt[:, 5:6], -0.5, 1.5,
                                       ALU.mult, ALU.add))
                    fv(v.tensor_tensor(mvt[:, 4:5], mvt[:, 4:5], mvt[:, 5:6], ALU.mult))
                v.tensor_scalar(xov, x1v, mvt[:, 0:1], mvt[:, 4:5],
                                ALU.subtract, ALU.mult)
                v.engine_nop().then_inc(swp, 1)

        @block.tensor
        def _(te):
            te.wait_ge(twd, 16 * NIC)
            for t in range(NTT):
                te.wait_ge(ldo, 16 * NIC * (t + 1))
                if t >= 2:
                    te.wait_ge(swp, (t - 1) if final else 3 * (t - 1))
                for ci in range(NIC):
                    ov = obt[:, (t % 2) * NIC * 128 + 128 * ci:
                             (t % 2) * NIC * 128 + 128 * ci + 128]
                    for h in range(2):
                        inst = te.matmul(banks[(t % 2) * 2 + h][:, :], ov,
                                         wdT[:, D * ci + 512 * h:D * ci + 512 * h + 512],
                                         start=(ci == 0), stop=(ci == NIC - 1))
                inst.then_inc(mmx, 1)

        @block.gpsimd
        def _(g):
            gfc = [0]

            def fg(inst):
                (inst if inst is not None else g.engine_nop()).then_inc(fsg, 1)
                gfc[0] += 1
                g.wait_ge(fsg, gfc[0])

            g.wait_ge(scl, 1)
            for c in range(NIC):
                g.wait_ge(ldp, 32 * (c + 1))
                if c >= 2:
                    g.wait_ge(spo, 16 * (c - 1))
                pv = p4in[:, (c % 2) * NT:(c % 2) * NT + NT]
                uv = u4in[:, (c % 2) * NT:(c % 2) * NT + NT]
                obv = ob[:, (c % 2) * NT:(c % 2) * NT + NT]
                sl = slice(SPLIT, NT)
                g.tensor_tensor(A[:, sl], pv[:, sl], uv[:, sl], ALU.mult)
                _emit_round_half(g, True, SPLIT, NT, A, r5P[:, 0:1], B, C, A, I1, OI)
                g.tensor_scalar(obv[:, sl], OI[:, sl], s5[:, 0:1], 0.0,
                                ALU.mult, ALU.add)
                g.engine_nop().then_inc(gD, 1)
    return None



def build_p5b(fz, io, stages=8, skip=()):
    nc = fz.nc
    xn2 = io["xn2"]
    x1 = io["x1"]
    cawT = io["cawT"]
    cpwT = io["cpwT"]
    cab24 = io["cab24"]
    cpbb = io["cpbb"]
    smallw = io["smallw"]
    wqkT2 = io["wqkT2"]
    l2oT2 = io["l2oT2"]
    l2obb = io["l2obb"]
    biases = io["biases"]
    cosT2 = io["cosT2"]
    sinT2 = io["sinT2"]
    masks = io["masks"]
    x2 = io["x2"]
    xn3 = io["xn3"]
    m1p = io["m1p"]
    vth = io["vth"]
    yhh = io["yhh"]

    NOC = 24
    NQC = 4
    NTL = T // 128
    RSC = 0.17677669529663687

    CPWN = 8 if stages >= 8 else 0

    def ktiles(qc):
        return 4 * qc + 4

    ltm_at_n, ltm_at_s, ltm_at_w = {}, {}, {}
    lte_after_p1, lte_after_p3, lte_free, lte_voc, lte_seq = {}, {}, {}, {}, {}
    lte_c, ltm_c = 2, 0
    for s in range(S_PC):
        for oc in range(NOC):
            gidx = s * NOC + oc
            ltm_c += 1
            ltm_at_n[gidx] = ltm_c
            if oc < 16:
                lte_c += 1
                lte_after_p1[gidx] = lte_c
                ltm_c += 1
                ltm_at_s[gidx] = ltm_c
                if oc < 8:
                    lte_c += 1
                    lte_after_p3[gidx] = lte_c
                    ltm_c += 1
                    ltm_at_w[gidx] = ltm_c
                    lte_c += 1
                else:
                    lte_c += 1
            else:
                lte_c += 1
                lte_voc[(s, oc - 16)] = lte_c
            lte_free[gidx] = lte_c
        lte_seq[s] = lte_c

    stm_at, mka_at, ste_at, avm_at, avm_grp = {}, {}, {}, {}, {}
    sT_wait, msk_wait, avm_hgrp, stm_end, avm_end = {}, {}, {}, {}, {}
    last_sT = [None] * 4
    last_msk = [0, 0]
    c_stm = c_mka = c_ste = c_avm = 0
    for s in range(S_PC):
        for h in range(NH):
            for qc in range(NQC):
                for kp in range(ktiles(qc) // 2):
                    diag = (2 * kp >= 4 * qc)
                    for kt in (2 * kp, 2 * kp + 1):
                        slot = 2 * (kp % 2) + (kt % 2)
                        sT_wait[(s, h, qc, kt)] = last_sT[slot]
                        c_stm += 1
                        stm_at[(s, h, qc, kt)] = c_stm
                    if diag:
                        msk_wait[(s, h, qc, kp)] = last_msk[kp % 2]
                        for kt in (2 * kp, 2 * kp + 1):
                            c_mka += 1
                            mka_at[(s, h, qc, kt)] = c_mka
                    c_ste += 1
                    ste_at[(s, h, qc, kp)] = c_ste
                    if diag:
                        last_msk[kp % 2] = c_ste
                    for kt in (2 * kp, 2 * kp + 1):
                        slot = 2 * (kp % 2) + (kt % 2)
                        last_sT[slot] = (('mka', mka_at[(s, h, qc, kt)]) if diag
                                         else ('ste', c_ste))
                    for kt in (2 * kp, 2 * kp + 1):
                        c_avm += 1
                        avm_at[(s, h, qc, kt)] = c_avm
                avm_grp[(s, h, qc)] = c_avm
            if h % 4 == 3:
                avm_hgrp[4 * s + h // 4] = c_avm
        stm_end[s] = c_stm
        avm_end[s] = c_avm

    with ExitStack() as es:
        def sb(name, shape, dt):
            return es.enter_context(nc.sbuf_tensor(fz.p + name, shape, dt))

        def sem(name):
            return fz.sem(name)

        bigT = sb("bigT", [128, NDC * T], BF16)
        caw_t = sb("caw_t", [128, 2 * NDC * 128], BF16)
        qkv_xo = sb("qkvxo", [128, T], BF16)
        qkvb = qkv_xo
        xot = qkv_xo
        rope_x2 = sb("ropex2", [128, 2 * D], F32)
        ropeA = rope_x2
        x2t = rope_x2
        qr_bf = sb("qr_bf", [64, T], BF16)
        qfk = [sb("qfk%d" % i, [128, T], BF16) for i in range(6)]
        krp = [sb("krp%d" % i, [128, T], BF16) for i in range(6)]
        yu = [sb("yu%d" % i, [128, T], BF16) for i in range(4)]
        onesr = yu[0][0:16, :]
        vtk = sb("vtk", [128, NTL * 144], BF16)
        pbuf = sb("pbuf", [128, 2 * 1024], BF16)
        sqt = pbuf
        mskd = sb("mskd", [128, 2048], F32)
        sqs = mskd[:, 0:D]
        yht = sb("yht", [128, 2 * D], BF16)
        cs = sb("cs", [64, T], F32)
        sn = sb("sn", [128, T], F32)
        maskb = sb("maskb", [128, 4 * 512], F32)
        cabt = sb("cabt", [128, 24], F32)
        cabh = sb("cabh", [128, 24], BF16)
        cabl = sb("cabl", [128, 24], BF16)
        ebias = sb("ebias", [64, 40], F32)
        cpbt = sb("cpbt", [128, D], F32)
        smw = sb("smw", [128, 256], BF16)
        wqk_b = sb("wqk_b", [64, 64], BF16)
        l2o_b = sb("l2o_b", [128, 128], BF16)
        l2obt = sb("l2obt", [128, 128], F32)
        bias_t = sb("bias_t", [64, 8], F32)
        ones128 = sb("ones128", [128, 1], BF16)
        ones1 = sb("ones1", [1, 128], F32)
        crow = sb("crow", [1, 32], F32)
        negcb1 = sb("negcb1", [128, 1], F32)
        zasm = sb("zasm", [32, T], F32)
        zrow = sb("zrow", [1, 1024], F32)
        rzt = sb("rzt", [128, NTL * 32], F32)
        x1t = sb("x1t", [128, 2 * D], F32)
        tr32 = x1t[0:32, 0:128]
        s1t = sb("s1t", [128, 1], F32)
        s2t = sb("s2t", [128, 1], F32)
        mvt = sb("mvt", [128, 8], F32)
        mxc = sb("mxc", [128, 40], F32)
        m11 = sb("m11", [1, 1], F32)
        bankA = es.enter_context(nc.psum_tensor(fz.p + "bankA", [128, 2048], F32))
        bankB = es.enter_context(nc.psum_tensor(fz.p + "bankB", [128, 2048], F32))

        def cpw_v(cj):
            return qfk[cj // 2][:, 1024 * (cj % 2):1024 * (cj % 2) + 1024]

        ldc = sem("ldc")
        tcast = sem("tcast")
        txn = sem("txn")
        qkm = sem("qkm")
        qke = sem("qke")
        ltm = sem("ltm")
        lte = sem("lte")
        svh = sem("svh")
        tvt = sem("tvt")
        nrm = sem("nrm")
        nrmp = sem("nrmp")
        nrm2 = sem("nrm2")
        stm = sem("stm")
        mka = sem("mka")
        ste = sem("ste")
        avm = sem("avm")
        ave = sem("ave")
        zcp = sem("zcp")
        zev = sem("zev")
        l2m = sem("l2m")
        l2e = sem("l2e")
        syh = sem("syh")
        tyh = sem("tyh")
        cpm = sem("cpm")
        swp = sem("swp")
        sx2 = sem("sx2")
        sxn = sem("sxn")
        ldx = sem("ldx")
        csc = sem("csc")
        fsv = sem("fsv")
        ebm = sem("ebm")
        don = sem("don")
        block = fz

        NCONST = 7

        @block.sync
        def _(sync):
            sync.dma_start(out=cs[:, :], in_=cosT2[:, :]).then_inc(ldc, 16)
            sync.dma_start(out=sn[0:64, :], in_=sinT2[:, :]).then_inc(ldc, 16)
            for dst, src_ in ((maskb[:, :], masks[:, :]), (cabt[:, :], cab24[:, :]),
                              (cpbt[:, :], cpbb[:, :]), (l2obt[:, :], l2obb[:, :]),
                              (bias_t[:, :], biases[:, :])):
                sync.dma_start(out=dst, in_=src_).then_inc(ldc, 16)
            sync.wait_ge(lte, 1)
            for sg in range(8):
                base = 144 * sg
                for j in range(4):
                    sync.dma_start(out=vth[base + 33 * j + 32:base + 33 * j + 33, :],
                                   in_=onesr[0:1, :]).then_inc(svh, 16)
                sync.dma_start(out=vth[base + 132:base + 144, :],
                               in_=onesr[4:16, :]).then_inc(svh, 16)
            for s in range(S_PC):
                if s == 1:
                    if stages >= 8:
                        sync.wait_ge(cpm, NTL)
                    else:
                        sync.wait_ge(qkm, 4 * NOC)
                for j in range(NDC):
                    sync.dma_start_transpose(bigT[:, j * T:(j + 1) * T],
                                             xn2[s * T:(s + 1) * T,
                                                 128 * j:128 * j + 128]).then_inc(txn, 16)
                if stages < 5:
                    continue
                sync.wait_ge(svh, 16 * (40 + (s + 1) * 16))

                def z_dmas(hlist, s=s):
                    for h in hlist:
                        for qc in range(NQC):
                            gi = 64 * s + 4 * h + qc
                            sync.wait_ge(zcp, gi + 1)
                            sync.dma_start(
                                out=zasm[h:h + 1, 512 * qc:512 * qc + 512],
                                in_=zrow[0:1, 512 * (gi % 2):
                                         512 * (gi % 2) + 512]).then_inc(zev, 16)

                for g in range(4):
                    if g >= 1:
                        z_dmas(range(4 * (g - 1), 4 * g))
                    if 4 * s + g >= 1:
                        sync.wait_ge(avm, avm_hgrp[4 * s + g - 1])
                    for tt in range(NTL):
                        sync.dma_start_transpose(
                            vtk[:, 144 * tt:144 * tt + 144],
                            vth[144 * (4 * s + g):144 * (4 * s + g) + 144,
                                128 * tt:128 * tt + 128]).then_inc(tvt, 16)
                z_dmas(range(12, 16))
                if stages < 6:
                    continue
                sync.wait_ge(syh, 16 * NTL * (s + 1))
                if stages < 8:
                    continue
                for j in range(NDC):
                    sync.dma_start_transpose(bigT[:, j * T:(j + 1) * T],
                                             yhh[s * T:(s + 1) * T,
                                                 128 * j:128 * j + 128]).then_inc(tyh, 16)
                for tl in range(NTL):
                    t = s * NTL + tl
                    if t >= 2:
                        sync.wait_ge(swp, 3 * (t - 1))
                    sync.dma_start(out=x1t[:, (t % 2) * D:(t % 2) * D + D],
                                   in_=x1[128 * t:128 * t + 128, :]).then_inc(ldx, 16)
            if stages >= 8:
                sync.wait_ge(don, 1)
                sync.dma_start(out=m1p[:, :], in_=m11[0:1, 0:1]).then_inc(sx2, 16)
                sync.wait_ge(sx2, 16 * NTT + 16)
                sync.wait_ge(sxn, 16 * NTT)
            else:
                sync.wait_ge(qke, 4 * NOC * S_PC)
                sync.wait_ge(lte, lte_seq[S_PC - 1])
                sync.wait_ge(svh, 16 * (40 + 16 * S_PC))
                if stages >= 4:
                    sync.wait_ge(nrm2, 15 * (S_PC - 1) + 15)
                if stages >= 5:
                    sync.wait_ge(zcp, 64 * S_PC)
                    sync.wait_ge(zev, 16 * 64 * S_PC)
                    sync.wait_ge(ave, NH * NQC * S_PC)
                if stages >= 6:
                    sync.wait_ge(syh, 16 * NTL * S_PC)

        @block.gpsimd
        def _(g):
            g.dma_start(out=smw[:, :], in_=smallw[:, :]).then_inc(tcast, 16)
            g.dma_start(out=wqk_b[:, :], in_=wqkT2[:, :]).then_inc(tcast, 16)
            g.dma_start(out=l2o_b[0:64, :], in_=l2oT2[:, :]).then_inc(tcast, 16)
            g.dma_start(out=l2o_b[64:128, :], in_=l2oT2[:, :]).then_inc(tcast, 16)
            for s in range(S_PC):
                for oc in range(NOC):
                    gidx = s * NOC + oc
                    if gidx >= 2:
                        g.wait_ge(qkm, 4 * (gidx - 1))
                    half = (gidx % 2) * NDC * 128
                    for j in range(NDC):
                        g.dma_start(out=caw_t[:, half + 128 * j:half + 128 * j + 128],
                                    in_=cawT[128 * j:128 * j + 128,
                                             128 * oc:128 * oc + 128]).then_inc(tcast, 16)
                if stages >= 8:
                    g.wait_ge(stm, stm_end[s])
                    for cj in range(NDC):
                        g.dma_start(out=cpw_v(cj),
                                    in_=cpwT[128 * cj:128 * cj + 128, :]).then_inc(tcast, 16)

        @block.scalar
        def _(sc):
            sc.wait_ge(ldc, 16 * NCONST)
            for s in range(S_PC):
                if s >= 1 and stages >= 8:
                    sc.wait_ge(sxn, 16 * NTL * s)
                for oc in range(NOC):
                    gidx = s * NOC + oc
                    for tc in range(4):
                        sc.wait_ge(qkm, 4 * gidx + tc + 1)
                        sc.activation(qkvb[:, 512 * tc:512 * tc + 512],
                                      bankA[:, 512 * tc:512 * tc + 512],
                                      ACT.Copy).then_inc(qke, 1)
                    if oc >= 16:
                        i = oc - 16
                        sc.wait_ge(lte, lte_voc[(s, i)])
                        for hh in range(2):
                            h = 2 * i + hh
                            gq, jq = h // 4, h % 4
                            base = 144 * (4 * s + gq) + 33 * jq
                            sc.dma_start(out=vth[base:base + 32, :],
                                         in_=qr_bf[32 * hh:32 * hh + 32, :]).then_inc(svh, 16)
                if stages < 4:
                    continue
                if s >= 1 and stages >= 5:
                    sc.wait_ge(avm, avm_end[s - 1])
                sc.wait_ge(lte, lte_seq[s])
                for ti in range(6):
                    for which, tiles in ((0, qfk), (1, krp)):
                        k = 2 * ti + which
                        sc.wait_ge(nrmp, 13 * s + k)
                        if "sq" in skip:
                            sc.nop().then_inc(nrm, 1)
                        else:
                            sc.activation(sqt[0:96, :], tiles[ti][0:96, :],
                                          ACT.Square).then_inc(nrm, 1)
                sc.wait_ge(nrm2, 15 * s + 13)
                if "sqrt" in skip:
                    sc.nop().then_inc(nrm, 1)
                else:
                    sc.activation(crow[0:1, 25:26], crow[0:1, 24:25],
                                  ACT.Sqrt).then_inc(nrm, 1)
                if stages < 5:
                    continue
                for h in range(NH):
                    for qc in range(NQC):
                        for kp in range(ktiles(qc) // 2):
                            kt1 = 2 * kp + 1
                            diag = (2 * kp >= 4 * qc)
                            if diag:
                                sc.wait_ge(mka, mka_at[(s, h, qc, kt1)])
                            else:
                                sc.wait_ge(stm, stm_at[(s, h, qc, kt1)])
                            if kp >= 2:
                                sc.wait_ge(avm, avm_at[(s, h, qc, 2 * kp - 3)])
                            src = (mskd if diag else bankA)
                            sc.activation(
                                pbuf[:, 1024 * (kp % 2):1024 * (kp % 2) + 1024],
                                src[:, 1024 * (kp % 2):1024 * (kp % 2) + 1024],
                                ACT.Exp, bias=negcb1[:, 0:1],
                                scale=RSC).then_inc(ste, 1)
                        sc.wait_ge(avm, avm_grp[(s, h, qc)])
                        gi = 64 * s + 4 * h + qc
                        if gi >= 2:
                            sc.wait_ge(zev, 16 * (gi - 1))
                        sc.activation(zrow[0:1, 512 * (gi % 2):512 * (gi % 2) + 512],
                                      bankB[32:33, 512 * (qc % 2):512 * (qc % 2) + 512],
                                      ACT.Copy).then_inc(zcp, 1)
                if stages < 6:
                    continue
                for tl in range(NTL):
                    sc.wait_ge(l2e, 8 * (NTL * s + tl) + 8)
                    sc.dma_start(out=yhh[s * T + 128 * tl:s * T + 128 * tl + 128, :],
                                 in_=yht[:, (tl % 2) * D:(tl % 2) * D + D]).then_inc(syh, 16)
                if stages < 8:
                    continue
                for tl in range(NTL):
                    t = s * NTL + tl
                    sc.wait_ge(swp, 3 * t + 1)
                    sc.dma_start(out=x2[128 * t:128 * t + 128, :],
                                 in_=x2t[:, (t % 2) * D:(t % 2) * D + D]).then_inc(sx2, 16)
                    sc.wait_ge(swp, 3 * t + 2)
                    sc.activation(mvt[:, 3:4], mvt[:, 2:3], ACT.Sqrt).then_inc(csc, 1)
                    sc.wait_ge(swp, 3 * t + 3)
                    sc.dma_start(out=xn3[128 * t:128 * t + 128, :],
                                 in_=xot[:, (t % 2) * D:(t % 2) * D + D]).then_inc(sxn, 16)

        @block.tensor
        def _(te):
            te.wait_ge(ldc, 16 * NCONST)
            te.wait_ge(lte, 1)
            te.wait_ge(tcast, 16 * 4)
            ents = ([(smw[:, 0:64], oc) for oc in range(8)] +
                    [(smw[:, 64:128], oc) for oc in range(8)] +
                    [(smw[:, 128:192], 8 + j) for j in range(8)] +
                    [(smw[:, 192:256], 8 + j) for j in range(8)] +
                    [(smw[:, 128:192], 16 + j) for j in range(8)])
            inst = None
            for e, (wsl_, cc) in enumerate(ents):
                te.matmul(bankB[0:64, 1792 + e:1793 + e], wsl_,
                          cabh[:, cc:cc + 1], start=True, stop=False)
                inst = te.matmul(bankB[0:64, 1792 + e:1793 + e], wsl_,
                                 cabl[:, cc:cc + 1], start=False, stop=True)
            inst.then_inc(ebm, 1)
            for s in range(S_PC):
                te.wait_ge(txn, 16 * NDC * (s + 1))
                if s >= 1:
                    if stages >= 8:
                        te.wait_ge(swp, 3 * NTL * s)
                    if stages >= 6:
                        te.wait_ge(l2e, 8 * NTL * s)
                    if stages >= 5:
                        te.wait_ge(zcp, 64 * s)
                for oc in range(NOC):
                    gidx = s * NOC + oc
                    te.wait_ge(tcast, 16 * (4 + (192 + CPWN) * s + 8 * (oc + 1)))
                    half = (gidx % 2) * NDC * 128
                    for tc in range(4):
                        if gidx >= 1:
                            te.wait_ge(qke, 4 * (gidx - 1) + tc + 1)
                        inst = None
                        for j in range(NDC):
                            inst = te.matmul(
                                bankA[:, 512 * tc:512 * tc + 512],
                                caw_t[:, half + 128 * j:half + 128 * j + 128],
                                bigT[:, j * T + 512 * tc:j * T + 512 * tc + 512],
                                start=(j == 0), stop=(j == NDC - 1))
                        inst.then_inc(qkm, 1)
                    te.wait_ge(qke, 4 * (gidx + 1))
                    if oc < 8:
                        wsl, wssl = smw[:, 0:64], smw[:, 64:128]
                    else:
                        wsl, wssl = smw[:, 128:192], smw[:, 192:256]
                    te.wait_ge(lte, lte_free[gidx - 1] if gidx >= 1 else 2)
                    inst = None
                    for tc in range(4):
                        inst = te.matmul(bankB[0:64, 512 * tc:512 * tc + 512], wsl,
                                         qkvb[:, 512 * tc:512 * tc + 512],
                                         start=True, stop=True)
                    inst.then_inc(ltm, 1)
                    if oc < 16:
                        te.wait_ge(lte, lte_after_p1[gidx])
                        inst = None
                        for tc in range(4):
                            inst = te.matmul(bankB[0:64, 512 * tc:512 * tc + 512],
                                             wssl, qkvb[:, 512 * tc:512 * tc + 512],
                                             start=True, stop=True)
                        inst.then_inc(ltm, 1)
                    if oc < 8:
                        te.wait_ge(lte, lte_after_p3[gidx])
                        inst = None
                        for tc in range(4):
                            inst = te.matmul(bankB[0:64, 512 * tc:512 * tc + 512],
                                             wqk_b[:, :],
                                             qr_bf[:, 512 * tc:512 * tc + 512],
                                             start=True, stop=True)
                        inst.then_inc(ltm, 1)
                if stages < 4:
                    continue
                te.wait_ge(qke, 4 * NOC * (s + 1))
                for k in range(12):
                    te.wait_ge(nrm, 13 * s + k + 1)
                    te.wait_ge(nrm2, 15 * s + k)
                    if "nmm" in skip:
                        te.nop().then_inc(nrmp, 1)
                        continue
                    inst = None
                    for tc in range(4):
                        inst = te.matmul(bankA[0:1, 512 * tc:512 * tc + 512],
                                         ones128[:, :],
                                         sqt[:, 512 * tc:512 * tc + 512],
                                         start=True, stop=True)
                    inst.then_inc(nrmp, 1)
                te.wait_ge(nrm2, 15 * s + 14)
                if "bmm" in skip:
                    te.nop().then_inc(nrmp, 1)
                else:
                    te.matmul(bankA[0:128, 0:1], ones1[:, :], crow[0:1, 26:27],
                              start=True, stop=True).then_inc(nrmp, 1)
                if stages < 5:
                    te.wait_ge(nrm2, 15 * s + 15)
                if stages < 5:
                    continue
                te.wait_ge(nrm2, 15 * s + 15)
                for h in range(NH):
                    gq, jq = h // 4, h % 4
                    te.wait_ge(tvt, 16 * NTL * (4 * s + gq + 1))
                    kti, kbase = h // 3, 32 * (h % 3)
                    for qc in range(NQC):
                        G = 64 * s + 4 * h + qc
                        if G >= 2:
                            te.wait_ge(ave, G - 1)
                            te.wait_ge(zcp, G - 1)
                        for kp in range(ktiles(qc) // 2):
                            for kt in (2 * kp, 2 * kp + 1):
                                w = sT_wait[(s, h, qc, kt)]
                                if w is not None:
                                    te.wait_ge(ste if w[0] == 'ste' else mka, w[1])
                                psl = 1024 * (kp % 2) + 512 * (kt % 2)
                                te.matmul(bankA[:, psl:psl + 512],
                                          krp[kti][kbase:kbase + 32,
                                                   128 * kt:128 * kt + 128],
                                          qfk[kti][kbase:kbase + 32,
                                                   512 * qc:512 * qc + 512],
                                          start=True, stop=True).then_inc(stm, 1)
                            if kp >= 1:
                                te.wait_ge(ste, ste_at[(s, h, qc, kp - 1)])
                                for kt in (2 * (kp - 1), 2 * (kp - 1) + 1):
                                    te.matmul(
                                        bankB[0:33, 512 * (qc % 2):512 * (qc % 2) + 512],
                                        vtk[:, 144 * kt + 33 * jq:144 * kt + 33 * jq + 33],
                                        pbuf[:, 1024 * ((kp - 1) % 2) + 512 * (kt % 2):
                                             1024 * ((kp - 1) % 2) + 512 * (kt % 2) + 512],
                                        start=(kt == 0), stop=False).then_inc(avm, 1)
                        kpl = ktiles(qc) // 2 - 1
                        te.wait_ge(ste, ste_at[(s, h, qc, kpl)])
                        for kt in (2 * kpl, 2 * kpl + 1):
                            te.matmul(
                                bankB[0:33, 512 * (qc % 2):512 * (qc % 2) + 512],
                                vtk[:, 144 * kt + 33 * jq:144 * kt + 33 * jq + 33],
                                pbuf[:, 1024 * (kpl % 2) + 512 * (kt % 2):
                                     1024 * (kpl % 2) + 512 * (kt % 2) + 512],
                                start=False, stop=(kt == 2 * kpl + 1)).then_inc(avm, 1)
                if stages < 6:
                    continue
                te.wait_ge(ave, NH * NQC * (s + 1))
                for tl in range(NTL):
                    for hp in range(8):
                        lidx = 8 * (NTL * s + tl) + hp
                        if lidx >= 2:
                            te.wait_ge(l2e, lidx - 1)
                        col = 1024 + 512 * (lidx % 2)
                        te.matmul(bankB[:, col:col + 128],
                                  yu[hp // 2][64 * (hp % 2):64 * (hp % 2) + 64,
                                              128 * tl:128 * tl + 128],
                                  l2o_b[64 * (hp % 2):64 * (hp % 2) + 64, :],
                                  start=True, stop=True).then_inc(l2m, 1)
                if stages < 8:
                    continue
                te.wait_ge(tcast, 16 * (4 + 200 * (s + 1)))
                te.wait_ge(tyh, 16 * NDC * (s + 1))
                for tl in range(NTL):
                    t = s * NTL + tl
                    if t >= 2:
                        te.wait_ge(swp, 3 * (t - 1))
                    col0 = (tl % 2) * 1024
                    inst = None
                    for cj in range(NDC):
                        for hh in range(2):
                            inst = te.matmul(
                                bankA[:, col0 + 512 * hh:col0 + 512 * hh + 512],
                                bigT[:, cj * T + 128 * tl:cj * T + 128 * tl + 128],
                                cpw_v(cj)[:, 512 * hh:512 * hh + 512],
                                start=(cj == 0), stop=(cj == NDC - 1))
                    inst.then_inc(cpm, 1)

        @block.vector
        def _(v):
            fv = Fence(v, fsv)
            fv(v.memset(yu[0][0:16, :], 0.0))
            v.memset(yu[0][0:4, :], 1.0)
            v.memset(ones1[:, :], 1.0)
            v.memset(ones128[0:96, :], 1.0)
            v.memset(ones128[96:128, :], 0.0)
            v.memset(qfk[5][32:64, :], 0.0)
            v.memset(qfk[5][64:96, :], 0.0)
            v.memset(krp[5][32:64, :], 0.0)
            v.memset(krp[5][64:96, :], 0.0)
            v.memset(pbuf[96:128, :], 0.0)
            v.memset(zasm[0:32, :], 0.0)
            fv(v.memset(mxc[:, 0:40], 0.0))
            v.wait_ge(ldc, 16 * NCONST)
            fv(v.tensor_copy(cabh[:, :], cabt[:, :]))
            fv(v.tensor_copy(mskd[:, 0:24], cabh[:, :]))
            fv(v.tensor_tensor(mskd[:, 24:48], cabt[:, :], mskd[:, 0:24],
                               ALU.subtract))
            fv(v.tensor_copy(cabl[:, :], mskd[:, 24:48]))
            v.engine_nop().then_inc(lte, 1)
            v.wait_ge(ebm, 1)
            for gi5, bcol in enumerate((0, 1, 2, 3, 2)):
                fv(v.tensor_scalar(ebias[:, 8 * gi5:8 * gi5 + 8],
                                   bankB[0:64, 1792 + 8 * gi5:1800 + 8 * gi5],
                                   bias_t[:, bcol:bcol + 1], None, ALU.add))
            v.engine_nop().then_inc(lte, 1)
            for s in range(S_PC):
                if s >= 1 and stages >= 8:
                    v.wait_ge(sx2, 16 * NTL * s)
                for oc in range(NOC):
                    gidx = s * NOC + oc
                    if oc < 8:
                        bq, bqs = ebias[:, oc:oc + 1], ebias[:, 8 + oc:9 + oc]
                    elif oc < 16:
                        bq = ebias[:, 8 + oc:9 + oc]
                        bqs = ebias[:, 16 + oc:17 + oc]
                    else:
                        bq = ebias[:, 16 + oc:17 + oc]
                    v.wait_ge(ltm, ltm_at_n[gidx])
                    if oc < 16:
                        v.tensor_scalar(ropeA[0:64, :], bankB[0:64, :],
                                        bq[0:64, 0:1], None, ALU.add)
                        v.tensor_tensor(ropeA[64:128, :], ropeA[0:64, :],
                                        cs[:, :], ALU.mult)
                        v.engine_nop().then_inc(lte, 1)
                        v.wait_ge(ltm, ltm_at_s[gidx])
                        v.tensor_scalar(ropeA[0:64, :], bankB[0:64, :],
                                        bqs[0:64, 0:1], None, ALU.add)
                        v.tensor_tensor(sn[64:128, :], ropeA[0:64, :],
                                        sn[0:64, :], ALU.mult)
                        if oc < 8:
                            v.tensor_tensor(qr_bf[0:64, :], ropeA[64:128, :],
                                            sn[64:128, :], ALU.add)
                            v.engine_nop().then_inc(lte, 1)
                            v.wait_ge(ltm, ltm_at_w[gidx])
                            for hh in range(2):
                                h = 2 * oc + hh
                                ti, tb = h // 3, 32 * (h % 3)
                                v.tensor_scalar(qfk[ti][tb:tb + 32, :],
                                                bankB[32 * hh:32 * hh + 32, :],
                                                bias_t[32 * hh:32 * hh + 32, 4:5],
                                                None, ALU.add)
                            v.engine_nop().then_inc(lte, 1)
                        else:
                            for hh in range(2):
                                h = 2 * (oc - 8) + hh
                                ti, tb = h // 3, 32 * (h % 3)
                                v.tensor_tensor(krp[ti][tb:tb + 32, :],
                                                ropeA[64 + 32 * hh:96 + 32 * hh, :],
                                                sn[64 + 32 * hh:96 + 32 * hh, :],
                                                ALU.add)
                            v.engine_nop().then_inc(lte, 1)
                    else:
                        i = oc - 16
                        if i > 0 or s > 0:
                            v.wait_ge(svh, 16 * (40 + 16 * s + 2 * i))
                        v.tensor_scalar(qr_bf[0:64, :], bankB[0:64, :],
                                        bq[0:64, 0:1], None, ALU.add)
                        v.engine_nop().then_inc(lte, 1)
                if stages < 4:
                    continue
                for k in range(12):
                    v.wait_ge(nrmp, 13 * s + k + 1)
                    ti, which = k // 2, k % 2
                    col = ti if which == 0 else 8 + ti
                    if "red" not in skip:
                        fv(v.tensor_reduce(crow[0:1, col:col + 1],
                                           bankA[0:1, 0:T], AXT.X, ALU.max))
                    v.engine_nop().then_inc(nrm2, 1)
                if "chain" not in skip:
                    if "red" in skip:
                        fv(v.memset(crow[0:1, 0:32], 1.0))
                    fv(v.tensor_tensor(crow[0:1, 16:22], crow[0:1, 0:6],
                                       crow[0:1, 8:14], ALU.mult))
                    fv(v.tensor_scalar(crow[0:1, 16:22], crow[0:1, 16:22],
                                       1.0 / LAT, None, ALU.mult))
                    fv(v.tensor_reduce(crow[0:1, 24:25], crow[0:1, 16:22],
                                       AXT.X, ALU.max))
                v.engine_nop().then_inc(nrm2, 1)
                v.wait_ge(nrm, 13 * s + 13)
                if "chain" not in skip:
                    fv(v.tensor_scalar(crow[0:1, 26:27], crow[0:1, 25:26],
                                       -1.0, None, ALU.mult))
                v.engine_nop().then_inc(nrm2, 1)
                v.wait_ge(nrmp, 13 * s + 13)
                if "negc" not in skip:
                    fv(v.tensor_copy(negcb1[:, 0:1], bankA[0:128, 0:1]))
                v.engine_nop().then_inc(nrm2, 1)
                if stages < 5:
                    continue
                for h in range(NH):
                    for qc in range(NQC):
                        for kt in range(4 * qc, ktiles(qc)):
                            kp = kt // 2
                            mw = msk_wait[(s, h, qc, kp)]
                            if mw:
                                v.wait_ge(ste, mw)
                            psl = 1024 * (kp % 2) + 512 * (kt % 2)
                            r = kt - 4 * qc
                            v.wait_ge(stm, stm_at[(s, h, qc, kt)])
                            v.tensor_tensor(mskd[:, psl:psl + 512],
                                            bankA[:, psl:psl + 512],
                                            maskb[:, 512 * r:512 * r + 512],
                                            ALU.add).then_inc(mka, 1)
                        v.wait_ge(avm, avm_grp[(s, h, qc)])
                        hp = h // 2
                        tb2 = 64 * (hp % 2) + 32 * (h % 2)
                        v.tensor_copy(yu[hp // 2][tb2:tb2 + 32,
                                                  512 * qc:512 * qc + 512],
                                      bankB[0:32, 512 * (qc % 2):
                                            512 * (qc % 2) + 512]).then_inc(ave, 1)
                if stages < 6:
                    continue
                for tl in range(NTL):
                    v.wait_ge(zev, 16 * (64 * s + 61 + tl // 4))
                    inst = None
                    for w in range(4):
                        inst = v.transpose(rzt[32 * w:32 * w + 32,
                                               32 * tl:32 * tl + 32],
                                           zasm[0:32, 128 * tl + 32 * w:
                                                128 * tl + 32 * w + 32])
                    fv(inst)
                    fv(v.tensor_scalar(rzt[:, 32 * tl:32 * tl + 16],
                                       rzt[:, 32 * tl:32 * tl + 16], 1.0, 1e-38,
                                       ALU.mult, ALU.add))
                    fv(v.reciprocal(rzt[:, 32 * tl:32 * tl + 16],
                                    rzt[:, 32 * tl:32 * tl + 16]))
                for tl in range(NTL):
                    gt = NTL * s + tl
                    if gt >= 2:
                        v.wait_ge(syh, 16 * (gt - 1))
                    for hp in range(8):
                        lidx = 8 * gt + hp
                        v.wait_ge(l2m, lidx + 1)
                        col = 1024 + 512 * (lidx % 2)
                        inst = None
                        for hh in range(2):
                            h = 2 * hp + hh
                            inst = v.scalar_tensor_tensor(
                                yht[:, (tl % 2) * D + 128 * hp + 64 * hh:
                                    (tl % 2) * D + 128 * hp + 64 * hh + 64],
                                bankB[:, col + 64 * hh:col + 64 * hh + 64],
                                rzt[:, 32 * tl + h:32 * tl + h + 1],
                                l2obt[:, 64 * hh:64 * hh + 64],
                                ALU.mult, ALU.add)
                        inst.then_inc(l2e, 1)
                if stages < 8:
                    continue
                for tl in range(NTL):
                    t = s * NTL + tl
                    v.wait_ge(cpm, NTL * s + tl + 1)
                    v.wait_ge(ldx, 16 * (t + 1))
                    if t >= 2:
                        v.wait_ge(sx2, 16 * (t - 1))
                        v.wait_ge(sxn, 16 * (t - 1))
                    col0 = (tl % 2) * 1024
                    x1v = x1t[:, (t % 2) * D:(t % 2) * D + D]
                    x2v = x2t[:, (t % 2) * D:(t % 2) * D + D]
                    xov = xot[:, (t % 2) * D:(t % 2) * D + D]
                    v.tensor_tensor(x2v, bankA[:, col0:col0 + 1024], cpbt[:, :],
                                    ALU.add)
                    v.tensor_tensor(x2v, x2v, x1v, ALU.add)
                    v.engine_nop().then_inc(swp, 1)
                    fv(v.reduce_sum(s1t[:, 0:1], x2v, AXT.X))
                    v.tensor_tensor(sqs[:, 0:D], x2v, x2v, ALU.mult)
                    fv(v.reduce_sum(s2t[:, 0:1], sqs[:, 0:D], AXT.X))
                    fv(v.tensor_scalar(mvt[:, 0:1], s1t[:, 0:1], 1.0 / D, None,
                                       ALU.mult))
                    fv(v.tensor_tensor(mvt[:, 1:2], mvt[:, 0:1], mvt[:, 0:1],
                                       ALU.mult))
                    fv(v.scalar_tensor_tensor(mvt[:, 2:3], s2t[:, 0:1], 1.0 / D,
                                              mvt[:, 1:2], ALU.mult, ALU.subtract))
                    fv(v.tensor_scalar(mvt[:, 2:3], mvt[:, 2:3], 1.0, LN_EPS,
                                       ALU.mult, ALU.add))
                    v.engine_nop().then_inc(swp, 1)
                    v.wait_ge(csc, t + 1)
                    fv(v.reciprocal(mvt[:, 4:5], mvt[:, 3:4]))
                    for _ in range(2):
                        fv(v.tensor_tensor(mvt[:, 5:6], mvt[:, 4:5], mvt[:, 4:5],
                                           ALU.mult))
                        fv(v.tensor_tensor(mvt[:, 5:6], mvt[:, 5:6], mvt[:, 2:3],
                                           ALU.mult))
                        fv(v.tensor_scalar(mvt[:, 5:6], mvt[:, 5:6], -0.5, 1.5,
                                           ALU.mult, ALU.add))
                        fv(v.tensor_tensor(mvt[:, 4:5], mvt[:, 4:5], mvt[:, 5:6],
                                           ALU.mult))
                    v.tensor_scalar(xov, x2v, mvt[:, 0:1], mvt[:, 4:5],
                                    ALU.subtract, ALU.mult)
                    fv(v.tensor_reduce(mxc[:, 1:2], xov, AXT.X, ALU.max,
                                       apply_absolute_value=True))
                    fv(v.tensor_tensor(mxc[:, 0:1], mxc[:, 0:1], mxc[:, 1:2],
                                       ALU.max))
                    v.engine_nop().then_inc(swp, 1)
            if stages >= 8:
                emit_colmax_to_scalar(v, mxc[:, 0:32], tr32, m11[0:1, 0:1],
                                      fence=fv)
                v.engine_nop().then_inc(don, 1)
    return (sx2, 16 * NTT + 16)



def _gather_parts(results, key):
    row = np.concatenate([np.asarray(results[i][key]).reshape(1, 1)
                          for i in range(NCORES)], axis=1)
    return np.ascontiguousarray(np.broadcast_to(row, (128, 8)).astype(np.float32))




def _attn_consts(inputs):
    f32 = np.float32
    caw = np.asarray(inputs["L0_attn_c_attn_w"], f32)
    cab = np.asarray(inputs["L0_attn_c_attn_b"], f32)
    q2l = np.asarray(inputs["L0_attn_q2l_w"], f32)
    q2lb = np.asarray(inputs["L0_attn_q2l_b"], f32)
    kvl = np.asarray(inputs["L0_attn_kv2l_w"], f32)
    kvlb = np.asarray(inputs["L0_attn_kv2l_b"], f32)
    l2o = np.asarray(inputs["L0_attn_l2o_w"], f32)
    l2ob = np.asarray(inputs["L0_attn_l2o_b"], f32)
    wqk = np.asarray(inputs["L0_attn_wqk_w"], f32)
    wqkb = np.asarray(inputs["L0_attn_wqk_b"], f32)
    cpw = np.asarray(inputs["L0_attn_cproj_w"], f32)
    cpb = np.asarray(inputs["L0_attn_cproj_b"], f32)

    swp32 = np.arange(LAT) ^ 1

    smallw = np.zeros((128, 256), f32)
    smallw[0:64, 0:32] = q2l.T
    smallw[64:128, 32:64] = q2l.T
    smallw[:, 64:128] = smallw[:, 0:64][:, np.concatenate([swp32, swp32 + 32])]
    smallw[0:64, 128 + 0:128 + 32] = kvl.T
    smallw[64:128, 128 + 32:128 + 64] = kvl.T
    smallw[:, 192:256] = smallw[:, 128:192][:, np.concatenate([swp32, swp32 + 32])]

    wqkT2 = np.zeros((64, 64), f32)
    wqkT2[0:32, 0:32] = wqk.T
    wqkT2[32:64, 32:64] = wqk.T

    l2oT2 = np.zeros((64, 128), f32)
    l2oT2[0:32, 0:64] = l2o.T
    l2oT2[32:64, 64:128] = l2o.T
    l2obb = np.ascontiguousarray(
        np.broadcast_to(np.tile(l2ob, 2)[None, :], (128, 128)).astype(f32))

    biases = np.zeros((64, 8), f32)
    biases[:, 0] = np.tile(q2lb, 2)
    biases[:, 1] = np.tile(q2lb[swp32], 2)
    biases[:, 2] = np.tile(kvlb, 2)
    biases[:, 3] = np.tile(kvlb[swp32], 2)
    biases[:, 4] = np.tile(wqkb, 2)

    inv_freq = (1.0 / 10000.0 ** (np.arange(0, LAT, 2, dtype=np.float64) / LAT))
    freqs = np.outer(np.arange(T, dtype=np.float64), inv_freq)
    cosf = np.repeat(np.cos(freqs), 2, axis=-1)[:, :LAT].astype(f32)
    sinf = np.repeat(np.sin(freqs), 2, axis=-1)[:, :LAT].astype(f32)
    sinT = sinf.T.copy()
    sinT[0::2, :] *= -1.0
    cosT2 = np.ascontiguousarray(np.concatenate([cosf.T, cosf.T], axis=0))
    sinT2 = np.ascontiguousarray(np.concatenate([sinT, sinT], axis=0))

    kl = np.arange(128)[:, None]
    ql = np.arange(512)[None, :]
    masks = np.concatenate(
        [np.where(128 * r + kl <= ql, 0.0, -1e30).astype(f32) for r in range(4)],
        axis=1)

    return {
        "cawT": np.ascontiguousarray(caw.T),
        "cpwT": np.ascontiguousarray(cpw.T),
        "cab24": np.ascontiguousarray(cab.reshape(24, 128).T),
        "cpbb": np.ascontiguousarray(np.broadcast_to(cpb[None, :], (128, D)).astype(f32)),
        "smallw": smallw,
        "wqkT2": wqkT2,
        "l2oT2": l2oT2,
        "l2obb": l2obb,
        "biases": biases,
        "cosT2": cosT2,
        "sinT2": sinT2,
        "masks": np.ascontiguousarray(masks),
    }


_PROGS = {}


def build_fused():
    nc = new_nc()

    def ext_in(name, shape, dt):
        return nc.dram_tensor(name, shape, dt, kind="ExternalInput").ap()

    def internal(name, shape, dt):
        return nc.dram_tensor(name, shape, dt, kind="Internal").ap()

    x0 = ext_in("x0", [NT, D], F32)
    wup = ext_in("wup", [DI, D], F32)
    wga = ext_in("wga", [DI, D], F32)
    wdo = ext_in("wdo", [D, DI], F32)
    cawT = ext_in("cawT", [D, 3 * D], F32)
    cpwT = ext_in("cpwT", [D, D], F32)
    cab24 = ext_in("cab24", [128, 24], F32)
    cpbb = ext_in("cpbb", [128, D], F32)
    smallw = ext_in("smallw", [128, 256], F32)
    wqkT2 = ext_in("wqkT2", [64, 64], F32)
    l2oT2 = ext_in("l2oT2", [64, 128], F32)
    l2obb = ext_in("l2obb", [128, 128], F32)
    biases = ext_in("biases", [64, 8], F32)
    cosT2 = ext_in("cosT2", [64, T], F32)
    sinT2 = ext_in("sinT2", [64, T], F32)
    masks = ext_in("masks", [128, 4 * 512], F32)
    xout = nc.dram_tensor("xout", [NT, D], F32, kind="ExternalOutput").ap()

    wupq = internal("wupq", [DI, D], BF16)
    wgaq = internal("wgaq", [DI, D], BF16)
    wdoq = internal("wdoq", [D, DI], BF16)
    xn1 = internal("xn1", [NT, D], BF16)
    upr = internal("upr", [DI, NT], BF16)
    gar = internal("gar", [DI, NT], BF16)
    g1s = internal("g1s", [DI, NT], BF16)
    i4h = internal("i4h", [DI, NT], I16)
    i2h = internal("i2h", [DI, NT], I16)
    xqh = internal("xqh", [NT, D], BF16)
    obh = internal("obh", [DI, NT], BF16)
    x1 = internal("x1", [NT, D], F32)
    xn2 = internal("xn2", [NT, D], BF16)
    x2 = internal("x2", [NT, D], F32)
    xn3 = internal("xn3", [NT, D], BF16)
    vth = internal("vth", [8 * 144, T], BF16)
    yhh = internal("yhh", [NT, D], BF16)

    with ExitStack() as ges:
        fz = FZ(nc, ges)
        fz.make_cc(10)
        emit_init(fz)

        st = build_p1(fz, dict(x0=x0, wup=wup, wga=wga, wdo=wdo, wupq=wupq,
                               wgaq=wgaq, wdoq=wdoq, xn1=xn1,
                               m1p=fz.cci[0].ap()[0:1, 0:1]))
        emit_cc(fz, st, [0])
        fz.end_phase()

        def ffn_pass(xn, m1_k, xres, xo, xno, k2, k3, k4, k5, final):
            st = build_p2(fz, dict(xn=xn, mparts=[fz.mpart(m1_k)], wuq=wupq,
                                   wgq=wgaq, upr=upr, gar=gar,
                                   m2p=fz.cci[k2].ap()[0:1, 0:1],
                                   m3p=fz.cci[k3].ap()[0:1, 0:1], xqh=xqh))
            emit_cc(fz, st, [k2, k3])
            fz.end_phase()
            st = build_p3(fz, dict(gar=gar, mparts=[fz.mpart(k3)], g1s=g1s,
                                   m4p=fz.cci[k4].ap()[0:1, 0:1]))
            emit_cc(fz, st, [k4])
            fz.end_phase()
            st = build_p4(fz, dict(g1s=g1s, upr=upr,
                                   mparts=[fz.mpart(k2), fz.mpart(k3),
                                           fz.mpart(k4)],
                                   m5p=fz.cci[k5].ap()[0:1, 0:1],
                                   i4h=i4h, i2h=i2h))
            emit_cc(fz, st, [k5])
            fz.end_phase()
            build_p5a(fz, dict(i4h=i4h, i2h=i2h,
                               mparts=[fz.mpart(k2), fz.mpart(k3),
                                       fz.mpart(k4), fz.mpart(k5)],
                               wdoq=wdoq, xres=xres, xout=xo, xnout=xno,
                               obh=obh), final)
            fz.end_phase()

        ffn_pass(xn1, 0, x0, x1, xn2, 1, 2, 3, 4, final=False)

        st = build_p5b(fz, dict(xn2=xn2, x1=x1, cawT=cawT, cpwT=cpwT,
                                cab24=cab24, cpbb=cpbb, smallw=smallw,
                                wqkT2=wqkT2, l2oT2=l2oT2, l2obb=l2obb,
                                biases=biases, cosT2=cosT2, sinT2=sinT2,
                                masks=masks, x2=x2, xn3=xn3,
                                m1p=fz.cci[5].ap()[0:1, 0:1],
                                vth=vth, yhh=yhh))
        emit_cc(fz, st, [5])
        fz.end_phase()

        ffn_pass(xn3, 5, x2, xout, None, 6, 7, 8, 9, final=True)
    return nc


def kernel(**inputs):
    x = np.asarray(inputs["L0_attn_x"], dtype=np.float32)
    w_gate = np.ascontiguousarray(np.asarray(inputs["L0_ffn_w_gate"], np.float32))
    w_up = np.ascontiguousarray(np.asarray(inputs["L0_ffn_w_up"], np.float32))
    w_down = np.ascontiguousarray(np.asarray(inputs["L0_ffn_w_down"], np.float32))
    xs = [np.ascontiguousarray(x[2 * i:2 * i + 2].reshape(NT, D))
          for i in range(NCORES)]
    ac = _attn_consts(inputs)

    if "fused" not in _PROGS:
        _PROGS["fused"] = build_fused()
    prog = _PROGS["fused"]
    r = run_bass_kernel_spmd(
        prog,
        [dict(ac, x0=xs[i], wup=w_up, wga=w_gate, wdo=w_down)
         for i in range(NCORES)],
        core_ids=list(range(NCORES))).results

    out = np.concatenate(
        [np.asarray(r[i]["xout"], np.float32).reshape(2, T, D)
         for i in range(NCORES)], axis=0)
    return np.ascontiguousarray(out)
